# version 17 (final)
# baseline (speedup 1.0000x reference)
import numpy as np
import ml_dtypes
import concourse.bass as bass
import concourse.mybir as mybir
from concourse.bass_utils import run_bass_kernel_spmd

F32 = mybir.dt.float32
BF16 = mybir.dt.bfloat16
AF = mybir.ActivationFunctionType
ALU = mybir.AluOpType
AX = mybir.AxisListType

D = 1024
T = 4096
HD = 64
NE = 8
HID = 4096
DC = D // 128
TT = T // 128
NPAIR = 8
IB = T // 512
NH = 4
QT = T // NH
BIGNEG = -1e30

ENGINES = ("tensor", "scalar", "vector", "sync", "gpsimd")
STAGE = 5
DEBUG = ()


class Plan:
    def __init__(self):
        self.ops = {e: [] for e in ENGINES}
        self.cnt = {}

    def bump(self, name, n=1):
        self.cnt[name] = self.cnt.get(name, 0) + n
        return self.cnt[name]

    def cur(self, name):
        return self.cnt.get(name, 0)

    def op(self, eng, build, waits=(), incs=()):
        planned = []
        for sem, name, n in incs:
            self.bump(name, n)
            planned.append((sem, n))
        waits = tuple((s, v) for (s, v) in waits if v > 0)

        def f(E, build=build, waits=waits, pincs=tuple(planned)):
            for s, v in waits:
                E.wait_ge(s, v)
            ins = build(E)
            for s, n in pincs:
                ins = ins.then_inc(s, n)

        self.ops[eng].append(f)

    def wait(self, eng, sem, val):
        if val > 0:
            self.ops[eng].append(lambda E, s=sem, v=val: E.wait_ge(s, v))


def _mask_tile(o):
    istart = 0 if o == 0 else 128 * o - 64
    N = 512 if o == 0 else 576 - 128 * o
    jl = np.arange(128)[:, None]
    ii = np.arange(N)[None, :]
    allowed = (128 * o + jl - istart - ii) <= 63
    return allowed.astype(np.float32).astype(ml_dtypes.bfloat16)


def build_nc(stage=None, debug=None):
    stage = STAGE if stage is None else stage
    debug = DEBUG if debug is None else debug
    nc = bass.Bass("TRN2", target_bir_lowering=False)
    p = Plan()
    ctx_stack = []

    def keep(g):
        ctx_stack.append(g)
        return g.__enter__()

    x_h = nc.dram_tensor("x", [T, D], F32, kind="ExternalInput").ap()
    w1ta_h = nc.dram_tensor("w1t_a", [D, D], F32, kind="ExternalInput").ap()
    b1a_h = nc.dram_tensor("b1_a", [128, DC], F32, kind="ExternalInput").ap()
    w2ta_h = nc.dram_tensor("w2t_a", [D, D], F32, kind="ExternalInput").ap()
    b2a_h = nc.dram_tensor("b2_a", [128, DC], F32, kind="ExternalInput").ap()
    wgt_h = nc.dram_tensor("wgt", [D, 3 * D], F32, kind="ExternalInput").ap()
    bgr_h = nc.dram_tensor("bg_r", [128, 24], F32, kind="ExternalInput").ap()
    wot_h = nc.dram_tensor("wot", [D, D], F32, kind="ExternalInput").ap()
    bo_h = nc.dram_tensor("bo_row", [1, D], F32, kind="ExternalInput").ap()
    rlns_h = nc.dram_tensor("rlns", [128, DC], F32, kind="ExternalInput").ap()
    rlnb_h = nc.dram_tensor("rlnb", [128, DC], F32, kind="ExternalInput").ap()
    rwt_h = nc.dram_tensor("rwt", [D, NE], F32, kind="ExternalInput").ap()
    rb_h = nc.dram_tensor("rb_row", [1, NE], F32, kind="ExternalInput").ap()
    elns_h = nc.dram_tensor("elns", [NE, 128, DC], F32, kind="ExternalInput").ap()
    elnb_h = nc.dram_tensor("elnb", [NE, 128, DC], F32, kind="ExternalInput").ap()
    w1tf_h = nc.dram_tensor("w1t_f", [NE, D, HID], F32, kind="ExternalInput").ap()
    b1f_h = nc.dram_tensor("b1_f", [NE, 128, HID // 128], F32, kind="ExternalInput").ap()
    w2tf_h = nc.dram_tensor("w2t_f", [NE, HID, D], F32, kind="ExternalInput").ap()
    b2f_h = nc.dram_tensor("b2_row", [1, NE * D], F32, kind="ExternalInput").ap()

    y_h = nc.dram_tensor("y", [T, D], F32, kind="ExternalOutput").ap()
    nrm_h = nc.dram_tensor("nrm_hbm", [128, DC, T], BF16, kind="Internal").ap()
    xnT_h = nc.dram_tensor("xnT_hbm", [128, DC, T], BF16, kind="Internal").ap()
    att_h = nc.dram_tensor("att_hbm", [T, D], BF16, kind="Internal").ap()

    ident_h = nc.inline_tensor(
        np.eye(128, dtype=np.float32).astype(ml_dtypes.bfloat16), "identc").ap()
    masks_h = [nc.inline_tensor(_mask_tile(o), f"maskc{o}").ap() for o in range(5)]

    dbg = {}
    def dbgten(name, shape, dtype):
        if name in debug:
            dbg[name] = nc.dram_tensor("dbg_" + name, shape, dtype, kind="ExternalOutput").ap()
    dbgten("xnT", [128, DC, T], BF16)
    dbgten("nrm", [128, DC, T], BF16)
    dbgten("a2T", [128, DC, T], BF16)
    dbgten("qkT", [128, 2, T], BF16)
    dbgten("vaug", [128, TT, 130], BF16)
    dbgten("unorm", [128, NPAIR, T], BF16)
    dbgten("att", [T, D], BF16)
    dbgten("gates", [128, TT, NE], F32)
    dbgten("p0s", [128, 2, 16], F32)
    dbgten("p0xb", [128, 2, D], F32)
    dbgten("p0xnf", [128, D], F32)
    dbgten("p0sqb", [128, D], F32)
    dbgten("p0xnbf", [128, 2, D], BF16)
    dbgten("xnbf0", [128, D], BF16)

    sems = {}
    for n in ["sx0", "sx1", "sw", "swa", "sa", "sv", "sp", "snrm0", "snrm1",
              "satt", "satt1", "swg", "swo", "sy", "sdum", "srn0", "srn1",
              "sxnt0", "sxnt1", "sxn0", "sxn1",
              "sqw0", "sqw1", "sfn", "sfp", "sfx0", "sfx1", "sfa0", "sfa1",
              "syo0", "syo1"]:
        sems[n] = keep(nc.semaphore(n))
    SX = [sems["sx0"], sems["sx1"]]
    SW, SA, SV, SP = sems["sw"], sems["sa"], sems["sv"], sems["sp"]
    SWA = sems["swa"]
    SNRM = [sems["snrm0"], sems["snrm1"]]
    SY = sems["sy"]

    def vop(build, waits=()):
        p.op("vector", build, waits=tuple(waits) + ((SV, p.cur("sv")),),
             incs=[(SV, "sv", 1)])
        return p.cur("sv")

    def sop(build, waits=()):
        p.op("scalar", build, waits=tuple(waits) + ((SA, p.cur("sa")),),
             incs=[(SA, "sa", 1)])
        return p.cur("sa")

    def dma(eng, semname, sem, build, waits=()):
        w = tuple(waits) + ((sem, p.cur(semname)),)
        p.op(eng, build, waits=w, incs=[(sem, semname, 16)])
        return p.cur(semname)

    ident = keep(nc.sbuf_tensor("ident", [128, 128], BF16))
    b1a = keep(nc.sbuf_tensor("b1a", [128, DC], F32))
    b2a = keep(nc.sbuf_tensor("b2a", [128, DC], F32))
    bgr = keep(nc.sbuf_tensor("bgr", [128, 24], F32))
    bgr_bf = keep(nc.sbuf_tensor("bgr_bf", [128, 8], BF16))
    ones_row = keep(nc.sbuf_tensor("ones_row", [1, 128], BF16))
    rlns_s = keep(nc.sbuf_tensor("rlns_s", [128, DC], F32))
    rlnb_s = keep(nc.sbuf_tensor("rlnb_s", [128, DC], F32))
    rwt_s = keep(nc.sbuf_tensor("rwt_s", [128, DC, NE], BF16))
    rb_s = keep(nc.sbuf_tensor("rb_s", [1, NE], BF16))
    gates = keep(nc.sbuf_tensor("gates", [128, TT, NE], F32))

    for fn in [
        lambda g: g.dma_start(ident[:], ident_h[:]),
        lambda g: g.dma_start(b1a[:], b1a_h[:]),
        lambda g: g.dma_start(b2a[:], b2a_h[:]),
        lambda g: g.dma_start(bgr[:], bgr_h[:]),
        lambda g: g.dma_start(bgr_bf[:], bgr_h[:, 16:24]),
        lambda g: g.dma_start(rlns_s[:], rlns_h[:]),
        lambda g: g.dma_start(rlnb_s[:], rlnb_h[:]),
        lambda g: g.dma_start(rwt_s[:], rwt_h.rearrange("(c p) e -> p c e", p=128)),
        lambda g: g.dma_start(rb_s[:], rb_h[:]),
    ]:
        dma("gpsimd", "sw", SW, fn)
    pers_loaded = p.cur("sw")
    p.op("vector", lambda v: v.memset(ones_row[:], 1.0), incs=[(SV, "sv", 1)])
    ones_ready = p.cur("sv")

    a2T = keep(nc.sbuf_tensor("a2T", [128, DC, T], BF16))
    w1ta = keep(nc.sbuf_tensor("w1ta", [128, DC, D], BF16))
    w2ta = keep(nc.sbuf_tensor("w2ta", [128, DC, D], BF16))
    a1bf = keep(nc.sbuf_tensor("a1bf", [128, DC, 512], BF16))
    xnst = keep(nc.sbuf_tensor("xnst", [128, 2, DC, 512], BF16))

    dma("gpsimd", "swa", SWA, lambda g: g.dma_start(
        w1ta[:], w1ta_h.rearrange("(c p) j -> p c j", p=128)))
    pa_w_loaded = dma("gpsimd", "swa", SWA, lambda g: g.dma_start(
        w2ta[:], w2ta_h.rearrange("(c p) j -> p c j", p=128)))

    p0_guards = []

    def sb0(name, shape, dtype):
        g = nc.sbuf_tensor(name, shape, dtype)
        p0_guards.append(g)
        return g.__enter__()

    xb4 = sb0("xb4", [128, 4, D], F32)
    xcf4 = sb0("xcf4", [128, 4, D], F32)
    sqb4 = sb0("sqb4", [128, 4, D], F32)
    xnbf4 = sb0("xnbf4", [128, 4, D], BF16)
    nrmbf4 = sb0("nrmbf4", [128, 4, D], BF16)
    nrmc = sb0("nrmc", [128, 2, DC, 128], BF16)
    st4 = sb0("st4", [128, 4, 12], F32)

    p0_psum_guards = []
    g = nc.psum_tensor("pt_xn0", [128, DC, 128], BF16)
    p0_psum_guards.append(g); pt_xn = [g.__enter__()]
    g = nc.psum_tensor("pt_nrm0", [128, DC, 128], BF16)
    p0_psum_guards.append(g); pt_nrm = [g.__enter__()]

    pa_guards = []
    g = nc.psum_tensor("pa1", [128, 2, 512], F32); pa_guards.append(g)
    pa1 = g.__enter__()
    g = nc.psum_tensor("pa2", [128, 2, 2, 512], F32); pa_guards.append(g)
    pa2 = g.__enter__()

    nrm_evac_done = [0] * TT
    nrm_dma_done = [0] * TT
    xn_evac_done = [0] * TT
    prev_xc2 = 0
    prev_rstd2 = 0
    prev_tr = 0
    a1_evac = [0, 0]
    pa2_evac = [0, 0]
    xnst_used = [0, 0]

    for bb in range(8):
        w = []
        if bb >= 1:
            w.append((SV, prev_xc2))
        p.op("sync", lambda sy, bb=bb: sy.dma_start(
            xb4[:], x_h[bb * 512:(bb + 1) * 512, :].rearrange(
                "(t p) d -> p t d", p=128)), waits=w, incs=[(SX[0], "sx0", 16)])
        x_loaded = p.cur("sx0")

        w = [(SX[0], x_loaded)]
        if bb >= 1:
            w.append((SA, prev_rstd2))
        vop(lambda v: v.reduce_sum(st4[:, :, 0:1], xb4[:], axis=AX.X), waits=w)
        vop(lambda v: v.tensor_scalar_mul(st4[:, :, 1:2], st4[:, :, 0:1], 1.0 / D))
        for i in range(4):
            vop(lambda v, i=i: v.tensor_scalar(
                xcf4[:, i, :], xb4[:, i, :], st4[:, i, 1:2], None, ALU.subtract))
        xc_done = p.cur("sv")
        sq_done = sop(lambda sc: sc.activation(sqb4[:], xcf4[:], AF.Square),
                      waits=[(SV, xc_done)])
        vop(lambda v: v.reduce_sum(st4[:, :, 2:3], sqb4[:], axis=AX.X),
            waits=[(SA, sq_done)])
        var_done = vop(lambda v: v.tensor_scalar(
            st4[:, :, 3:4], st4[:, :, 2:3], 1.0 / D, 1e-5, ALU.mult, ALU.add))
        sop(lambda sc: sc.activation(st4[:, :, 4:5], st4[:, :, 3:4], AF.Ln),
            waits=[(SV, var_done)])
        rstd_done = sop(lambda sc: sc.activation(
            st4[:, :, 5:6], st4[:, :, 4:5], AF.Exp, scale=-0.5))
        for i in range(4):
            vop(lambda v, i=i: v.tensor_scalar(
                xb4[:, i, :], xcf4[:, i, :], st4[:, i, 5:6], None, ALU.mult),
                waits=([(SA, rstd_done)] if i == 0 else ()))
        w = []
        if bb >= 1:
            w.append((SP, prev_tr))
        xnbf_done = vop(lambda v: v.tensor_copy(xnbf4[:], xb4[:]), waits=w)
        vop(lambda v: v.reduce_sum(st4[:, :, 6:7], xb4[:], axis=AX.X))
        vop(lambda v: v.tensor_scalar_mul(st4[:, :, 7:8], st4[:, :, 6:7], 1.0 / D))
        for i in range(4):
            vop(lambda v, i=i: v.tensor_scalar(
                xcf4[:, i, :], xb4[:, i, :], st4[:, i, 7:8], None, ALU.subtract))
        xc2_done = p.cur("sv")
        prev_xc2 = xc2_done
        sq2_done = sop(lambda sc: sc.activation(sqb4[:], xcf4[:], AF.Square),
                       waits=[(SV, xc2_done)])
        vop(lambda v: v.reduce_sum(st4[:, :, 8:9], sqb4[:], axis=AX.X),
            waits=[(SA, sq2_done)])
        var2_done = vop(lambda v: v.tensor_scalar(
            st4[:, :, 9:10], st4[:, :, 8:9], 1.0 / D, 1e-6, ALU.mult, ALU.add))
        sop(lambda sc: sc.activation(st4[:, :, 10:11], st4[:, :, 9:10], AF.Ln),
            waits=[(SV, var2_done)])
        rstd2_done = sop(lambda sc: sc.activation(
            st4[:, :, 11:12], st4[:, :, 10:11], AF.Exp, scale=-0.5))
        prev_rstd2 = rstd2_done
        for i in range(4):
            vop(lambda v, i=i: v.tensor_scalar(
                nrmbf4[:, i, :], xcf4[:, i, :], st4[:, i, 11:12], None, ALU.mult),
                waits=([(SA, rstd2_done)] if i == 0 else ()))
        nrm_done = p.cur("sv")

        spar = bb % 2
        for i in range(4):
            tt = bb * 4 + i
            par = tt % 2
            w = [(SV, xnbf_done)]
            if tt == 0:
                w.append((SW, pers_loaded))
            if tt >= 1:
                w.append((SV, xn_evac_done[tt - 1]))
            if i == 0 and xnst_used[spar]:
                w.append((SP, xnst_used[spar]))
            for c in range(DC):
                p.op("tensor", lambda t, c=c, i=i: t.transpose(
                    pt_xn[0][:, c, :], xnbf4[:, i, c * 128:(c + 1) * 128],
                    ident[:]),
                    waits=(w if c == 0 else ()),
                    incs=([(SP, "sp", 1)] if c == DC - 1 else ()))
            trxn_done = p.cur("sp")
            w = [(SV, nrm_done)]
            if tt >= 1:
                w.append((SA, nrm_evac_done[tt - 1]))
            for c in range(DC):
                p.op("tensor", lambda t, c=c, i=i: t.transpose(
                    pt_nrm[0][:, c, :], nrmbf4[:, i, c * 128:(c + 1) * 128],
                    ident[:]),
                    waits=(w if c == 0 else ()),
                    incs=([(SP, "sp", 1)] if c == DC - 1 else ()))
            trnrm_done = p.cur("sp")
            xn_evac_done[tt] = vop(lambda v, spar=spar, i=i: v.tensor_copy(
                xnst[:, spar, :, i * 128:(i + 1) * 128], pt_xn[0][:]),
                waits=[(SP, trxn_done)])
            w = [(SP, trnrm_done)]
            if tt >= 2:
                w.append((SNRM[par], nrm_dma_done[tt - 2]))
            nrm_evac_done[tt] = sop(lambda sc, par=par: sc.activation(
                nrmc[:, par, :, :], pt_nrm[0][:], AF.Copy), waits=w)
            nrm_dma_done[tt] = dma("gpsimd", "snrm%d" % par, SNRM[par],
                lambda gy, tt=tt, par=par: gy.dma_start(
                    nrm_h[:, :, tt * 128:(tt + 1) * 128], nrmc[:, par, :, :]),
                waits=[(SA, nrm_evac_done[tt])])
        prev_tr = p.cur("sp")
        xn_batch_done = p.cur("sv")

        t0 = bb * 512
        for jc in range(DC):
            sl = jc % 2
            w = [(SV, xn_batch_done)]
            if bb == 0 and jc == 0:
                w.append((SWA, pa_w_loaded))
            if a1_evac[sl]:
                w.append((SA, a1_evac[sl]))
            for kc in range(DC):
                p.op("tensor", lambda t, sl=sl, jc=jc, kc=kc, spar=spar: t.matmul(
                    pa1[:, sl, :], w1ta[:, kc, jc * 128:(jc + 1) * 128],
                    xnst[:, spar, kc, :],
                    start=(kc == 0), stop=(kc == DC - 1)),
                    waits=(w if kc == 0 else ()),
                    incs=([(SP, "sp", 1)] if kc == DC - 1 else ()))
            mm1_done = p.cur("sp")
            a1_evac[sl] = sop(lambda sc, sl=sl, jc=jc: sc.activation(
                a1bf[:, jc, :], pa1[:, sl, :], AF.Silu, bias=b1a[:, jc:jc + 1]),
                waits=[(SP, mm1_done)])
        xnst_used[spar] = p.cur("sp")
        a1_ready = p.cur("sa")
        for q in range(4):
            pp = q % 2
            w = [(SA, a1_ready)]
            if pa2_evac[pp]:
                w.append((SA, pa2_evac[pp]))
            first = True
            for jc in range(DC):
                for d2 in range(2):
                    dc = 2 * q + d2
                    p.op("tensor", lambda t, pp=pp, d2=d2, dc=dc, jc=jc: t.matmul(
                        pa2[:, pp, d2, :], w2ta[:, jc, dc * 128:(dc + 1) * 128],
                        a1bf[:, jc, :],
                        start=(jc == 0), stop=(jc == DC - 1)),
                        waits=(w if first else ()),
                        incs=([(SP, "sp", 1)] if (jc == DC - 1 and d2 == 1) else ()))
                    first = False
            mm2_done = p.cur("sp")
            for d2 in range(2):
                dc = 2 * q + d2
                p.op("scalar", lambda sc, pp=pp, d2=d2, dc=dc, t0=t0: sc.activation(
                    a2T[:, dc, t0:t0 + 512], pa2[:, pp, d2, :], AF.Identity,
                    bias=b2a[:, dc:dc + 1]),
                    waits=([(SP, mm2_done)] if d2 == 0 else ()),
                    incs=([(SA, "sa", 1)] if d2 == 1 else ()))
            pa2_evac[pp] = p.cur("sa")

    p0_sa, p0_sv, p0_sp = p.cur("sa"), p.cur("sv"), p.cur("sp")
    pa_sa, pa_sp = p0_sa, p0_sp
    pb_early = [(SA, prev_rstd2), (SV, p0_sv), (SX[0], p.cur("sx0"))]

    if "nrm" in dbg:
        dma("sync", "sy", SY, lambda s: s.dma_start(dbg["nrm"][:], nrm_h[:]),
            waits=[(SNRM[0], p.cur("snrm0")), (SNRM[1], p.cur("snrm1"))])

    for g in reversed(pa_guards):
        g.__exit__(None, None, None)
    for g in reversed(p0_psum_guards):
        g.__exit__(None, None, None)

    if "a2T" in dbg:
        dma("sync", "sy", SY, lambda s: s.dma_start(dbg["a2T"][:], a2T[:]),
            waits=[(SA, pa_sa)])

    if stage >= 3:
        for g in reversed(p0_guards):
            g.__exit__(None, None, None)
        p0_guards.clear()
        for _ in range(4):
            g = ctx_stack.pop()
            g.__exit__(None, None, None)

        unorm = keep(nc.sbuf_tensor("unorm", [128, NPAIR, T], BF16))
        wot_sb = keep(nc.sbuf_tensor("wot_sb", [128, DC, D], BF16))
        cvec2 = keep(nc.sbuf_tensor("cvec2", [1, D], BF16))
        bo_sb = keep(nc.sbuf_tensor("bo_sb", [1, D], F32))

        pb_guards = []

        def sbB(name, shape, dtype):
            g = nc.sbuf_tensor(name, shape, dtype)
            pb_guards.append(g)
            return g.__enter__()

        masks_s = sbB("masks_s", [128, 5, 512], BF16)
        wgs = sbB("wgs", [128, 3, DC, 128], BF16)
        qTb = sbB("qTb", [128, T], BF16)
        kTb = sbB("kTb", [128, T], BF16)
        vaug = sbB("vaug", [128, TT, 130], BF16)
        pbuf = sbB("pbuf", [128, 2, 3, 512], BF16)
        usbf = sbB("usbf", [128, 2, 512], F32)
        recd = sbB("recd", [1, 2, 512], BF16)
        lnrow = sbB("lnrow", [1, 2, 512], F32)
        zrow = sbB("zrow", [1, 512], BF16)

        pS_g = nc.psum_tensor("pS", [128, 6, 512], F32); pb_guards.append(pS_g)
        pS = pS_g.__enter__()
        pU0_g = nc.psum_tensor("pU0", [128, 512], F32); pb_guards.append(pU0_g)
        pU1_g = nc.psum_tensor("pU1", [128, 512], F32); pb_guards.append(pU1_g)
        pUs = [pU0_g.__enter__(), pU1_g.__enter__()]

        SWG, SWO = sems["swg"], sems["swo"]
        fence_pa = [(SA, pa_sa), (SP, pa_sp), (SV, p0_sv),
                    (SX[0], p.cur("sx0")), (SX[1], p.cur("sx1")),
                    (SNRM[0], p.cur("snrm0")), (SNRM[1], p.cur("snrm1"))]
        mld = None
        for o in range(5):
            N = 512 if o == 0 else 576 - 128 * o
            mld = dma("gpsimd", "swg", SWG,
                      lambda g, o=o, N=N: g.dma_start(masks_s[:, o, 0:N], masks_h[o][:]),
                      waits=(pb_early if o == 0 else ()))
        masks_ld = mld
        dma("gpsimd", "swo", SWO, lambda g: g.dma_start(
            wot_sb[:], wot_h.rearrange("(c p) d -> p c d", p=128)),
            waits=pb_early)
        wot_ld = dma("gpsimd", "swo", SWO, lambda g: g.dma_start(bo_sb[:], bo_h[:]))

        vop(lambda v: v.memset(zrow[:], 0.0), waits=fence_pa)
        ones_cols = vop(lambda v: v.memset(vaug[:, :, 64:65], 1.0))
        ones_cols = vop(lambda v: v.memset(vaug[:, :, 129:130], 1.0))

        unorm_done = [[0] * IB for _ in range(NPAIR)]
        exp_last = {}
        av_last = {}
        pS_reader = {}
        pU_reader = [0, 0]
        bcast_done = [0, 0]
        qk_evac = {0: 0, 1: 0, 4: 0, 5: 0}
        wgs_used = 0

        for pp_i in range(NPAIR):
            base_q = pp_i * 128
            w = [(SP, wgs_used)] if pp_i > 0 else []
            for si, coff in enumerate([base_q, D + base_q, 2 * D + base_q]):
                wg_ld = dma("gpsimd", "swg", SWG,
                            lambda g, si=si, coff=coff: g.dma_start(
                                wgs[:, si, :, :],
                                wgt_h[:, coff:coff + 128].rearrange(
                                    "(c p) n -> p c n", p=128)),
                            waits=(w if si == 0 else ()))
            for qk in range(2):
                dstT = qTb if qk == 0 else kTb
                for tb in range(8):
                    sl = (qk * 8 + tb) % 2
                    w = [(SWG, wg_ld)]
                    if qk_evac[sl]:
                        w.append((SA, qk_evac[sl]))
                    if pp_i > 0 and qk == 0 and tb <= 1 and bcast_done[sl]:
                        w.append((SV, bcast_done[sl]))
                    if pp_i == 0 and tb == 0:
                        w += fence_pa
                    for kc in range(DC):
                        p.op("tensor", lambda t, sl=sl, qk=qk, kc=kc, tb=tb: t.matmul(
                            pS[:, sl, :], wgs[:, qk, kc, :],
                            a2T[:, kc, tb * 512:(tb + 1) * 512],
                            start=(kc == 0), stop=(kc == DC - 1)),
                            waits=(w if kc == 0 else ()),
                            incs=([(SP, "sp", 1)] if kc == DC - 1 else ()))
                    mm_done = p.cur("sp")
                    qk_evac[sl] = sop(
                        lambda sc, sl=sl, tb=tb, dstT=dstT, qk=qk, pp_i=pp_i: sc.activation(
                            dstT[:, tb * 512:(tb + 1) * 512], pS[:, sl, :],
                            AF.Identity, bias=bgr[:, qk * 8 + pp_i:qk * 8 + pp_i + 1]),
                        waits=[(SP, mm_done)])
            for jt in range(TT):
                sl = 4 + (jt % 2)
                w = [(SWG, wg_ld), (SV, ones_cols)]
                if qk_evac[sl]:
                    w.append((SA, qk_evac[sl]))
                if pp_i > 0 and jt <= 1:
                    prr = pS_reader.get((jt % 2, 2))
                    if prr:
                        w.append((SA if prr[0] == "sa" else SV, prr[1]))
                if pp_i == 0 and jt <= 1:
                    w += [(SA, pa_sa), (SV, p0_sv)]
                for kc in range(DC):
                    p.op("tensor", lambda t, sl=sl, kc=kc, jt=jt: t.matmul(
                        pS[:, sl, 0:128], a2T[:, kc, jt * 128:(jt + 1) * 128],
                        wgs[:, 2, kc, :],
                        start=(kc == 0), stop=(kc == DC - 1)),
                        waits=(w if kc == 0 else ()),
                        incs=([(SP, "sp", 1)] if kc == DC - 1 else ()))
                mm_done = p.cur("sp")
                sop(lambda sc, sl=sl, jt=jt: sc.activation(
                    vaug[:, jt, 0:64], pS[:, sl, 0:64], AF.Copy),
                    waits=[(SP, mm_done)])
                qk_evac[sl] = sop(lambda sc, sl=sl, jt=jt: sc.activation(
                    vaug[:, jt, 65:129], pS[:, sl, 64:128], AF.Copy))
            wgs_used = p.cur("sp")
            qkv_evac_done = p.cur("sa")

            for ib in range(IB):
                jmax = min(4 * ib + 4, TT - 1)

                def emit_S(J):
                    o = J - 4 * ib
                    istart = 0 if o <= 0 else 128 * o - 64
                    N = 512 if o <= 0 else 576 - 128 * o
                    for h in range(2):
                        sl = 2 * (J % 3) + h
                        w = [(SA, qkv_evac_done)]
                        pr = pS_reader.get((h, J % 3))
                        if pr:
                            w.append((SA if pr[0] == "sa" else SV, pr[1]))
                        p.op("tensor", lambda t, h=h, sl=sl, J=J, istart=istart,
                             N=N, ib=ib: t.matmul(
                            pS[:, sl, 0:N],
                            kTb[h * 64:(h + 1) * 64, J * 128:(J + 1) * 128],
                            qTb[h * 64:(h + 1) * 64,
                                ib * 512 + istart:ib * 512 + istart + N],
                            start=True, stop=True,
                            tile_position=(64 * h, 0)),
                            waits=w, incs=[(SP, "sp", 1)])

                def emit_exp(J):
                    o = J - 4 * ib
                    istart = 0 if o <= 0 else 128 * o - 64
                    N = 512 if o <= 0 else 576 - 128 * o
                    sp_done = p.cur("sp")
                    w2 = [(SP, sp_done)]
                    for h in range(2):
                        if av_last.get((h, J % 3), 0):
                            w2.append((SP, av_last[(h, J % 3)]))
                    if o < 0:
                        e_done = sop(lambda sc, J=J: sc.activation(
                            pbuf[:, :, J % 3, :], pS[:, 2 * (J % 3):2 * (J % 3) + 2, :],
                            AF.Exp, scale=0.125), waits=w2)
                        for h in range(2):
                            exp_last[(h, J % 3)] = ("sa", e_done)
                            pS_reader[(h, J % 3)] = ("sa", e_done)
                    else:
                        for h in range(2):
                            e_done = sop(lambda sc, h=h, J=J, N=N: sc.activation(
                                pbuf[:, h, J % 3, 0:N], pS[:, 2 * (J % 3) + h, 0:N],
                                AF.Exp, scale=0.125), waits=w2)
                            e_done2 = vop(lambda v, h=h, J=J, o=o, N=N: v.tensor_mul(
                                pbuf[:, h, J % 3, 0:N], pbuf[:, h, J % 3, 0:N],
                                masks_s[:, o, 0:N]),
                                waits=[(SA, e_done), (SWG, masks_ld)])
                            exp_last[(h, J % 3)] = ("sv", e_done2)
                            pS_reader[(h, J % 3)] = ("sa", e_done)

                def emit_AV(J):
                    o = J - 4 * ib
                    istart = 0 if o <= 0 else 128 * o - 64
                    N = 512 if o <= 0 else 576 - 128 * o
                    for h in range(2):
                        kind, val = exp_last[(h, J % 3)]
                        w = [(SV if kind == "sv" else SA, val)]
                        if J == 0 and pU_reader[h]:
                            k2, v2 = pU_reader[h]
                            w.append((SA if k2 == "sa" else SV, v2))
                        p.op("tensor", lambda t, h=h, J=J, istart=istart, N=N: t.matmul(
                            pUs[h][0:65, istart:istart + N],
                            vaug[:, J, h * 65:h * 65 + 65],
                            pbuf[:, h, J % 3, 0:N],
                            start=(J == 0), stop=False),
                            waits=w, incs=[(SP, "sp", 1)])
                        av_last[(h, J % 3)] = p.cur("sp")

                for J in range(jmax + 1):
                    emit_S(J)
                    emit_exp(J)
                    if J >= 2:
                        emit_AV(J - 2)
                emit_AV(jmax - 1)
                emit_AV(jmax)
                for h in range(2):
                    p.op("tensor", lambda t, h=h: t.matmul(
                        pUs[h][0:65, :], ones_row[0:1, 0:65], zrow[:],
                        start=False, stop=True), incs=[(SP, "sp", 1)])
                    av_last[(h, "close")] = p.cur("sp")
                u_done = [av_last[(0, "close")], av_last[(1, "close")]]

                for h in range(2):
                    ue_done = sop(lambda sc, h=h: sc.activation(
                        usbf[0:65, h, :], pUs[h][0:65, :], AF.Copy),
                        waits=[(SP, u_done[h])])
                    pU_reader[h] = ("sa", ue_done)
                    sop(lambda sc, h=h: sc.activation(
                        lnrow[0:1, h, :], usbf[64:65, h, :], AF.Ln))
                    r_done = sop(lambda sc, h=h: sc.activation(
                        recd[0:1, h, :], lnrow[0:1, h, :], AF.Exp, scale=-1.0))
                    w = [(SA, r_done), (SV, ones_ready)]
                    pr = pS_reader.get((h, 0))
                    if pr:
                        w.append((SA if pr[0] == "sa" else SV, pr[1]))
                    if bcast_done[h]:
                        w.append((SV, bcast_done[h]))
                    p.op("tensor", lambda t, h=h: t.matmul(
                        pS[:, h, :], ones_row[:], recd[0:1, h, :],
                        start=True, stop=True), waits=w, incs=[(SP, "sp", 1)])
                    b_done = p.cur("sp")
                    un = vop(lambda v, h=h, pp_i=pp_i, ib=ib: v.tensor_mul(
                        unorm[h * 64:(h + 1) * 64, pp_i, ib * 512:(ib + 1) * 512],
                        usbf[0:64, h, :], pS[0:64, h, :]),
                        waits=[(SP, b_done)])
                    bcast_done[h] = un
                    pS_reader[(h, 0)] = ("sv", un)
                unorm_done[pp_i][ib] = p.cur("sv")

        pb_sa, pb_sv, pb_sp = p.cur("sa"), p.cur("sv"), p.cur("sp")
        for g in reversed(pb_guards):
            g.__exit__(None, None, None)

        if "unorm" in dbg:
            dma("sync", "sy", SY, lambda s: s.dma_start(dbg["unorm"][:], unorm[:]),
                waits=[(SV, pb_sv)])
        if "qkT" in dbg:
            dma("sync", "sy", SY, lambda s: s.dma_start(dbg["qkT"][:, 0, :], qTb[:]),
                waits=[(SV, pb_sv), (SA, pb_sa)])
            dma("sync", "sy", SY, lambda s: s.dma_start(dbg["qkT"][:, 1, :], kTb[:]))
            dma("sync", "sy", SY, lambda s: s.dma_start(dbg["vaug"][:], vaug[:]))

        pc_guards = []
        attb_g = nc.sbuf_tensor("attb", [128, 2, D], BF16); pc_guards.append(attb_g)
        attb = attb_g.__enter__()
        pw_g = nc.psum_tensor("pw", [128, 2, 512], F32); pc_guards.append(pw_g)
        pw = pw_g.__enter__()
        pcv_g = nc.psum_tensor("pcv", [128, 512], F32); pc_guards.append(pcv_g)
        pcv = pcv_g.__enter__()

        SATT2 = [sems["satt"], sems["satt1"]]
        cv_done = 0
        for dch in range(2):
            w = [(SWO, wot_ld), (SA, pb_sa), (SV, pb_sv), (SW, pers_loaded)]
            if cv_done:
                w.append((SV, cv_done))
            for c in range(DC):
                p.op("tensor", lambda t, c=c, dch=dch: t.matmul(
                    pcv[0:1, :], bgr_bf[:, c:c + 1],
                    wot_sb[:, c, dch * 512:(dch + 1) * 512],
                    start=(c == 0), stop=(c == DC - 1)),
                    waits=(w if c == 0 else ()),
                    incs=([(SP, "sp", 1)] if c == DC - 1 else ()))
            cvm = p.cur("sp")
            cv_done = vop(lambda v, dch=dch: v.tensor_add(
                cvec2[0:1, dch * 512:(dch + 1) * 512], bo_sb[0:1, dch * 512:(dch + 1) * 512],
                pcv[0:1, 0:512]), waits=[(SP, cvm)])

        att_dma = [0, 0]
        attb_evac = [0, 0]
        satt_names = ["satt", "satt1"]
        for tt2 in range(TT):
            par = tt2 % 2
            for dch in range(2):
                w = [(SV, unorm_done[NPAIR - 1][tt2 // 4]), (SV, cv_done)]
                if attb_evac[par]:
                    w.append((SA, attb_evac[par]))
                for pp2 in range(NPAIR):
                    p.op("tensor", lambda t, pp2=pp2, tt2=tt2, dch=dch: t.matmul(
                        pw[:, dch, :], unorm[:, pp2, tt2 * 128:(tt2 + 1) * 128],
                        wot_sb[:, pp2, dch * 512:(dch + 1) * 512],
                        start=(pp2 == 0), stop=False),
                        waits=(w if pp2 == 0 else ()))
                p.op("tensor", lambda t, dch=dch: t.matmul(
                    pw[:, dch, :], ones_row[:], cvec2[0:1, dch * 512:(dch + 1) * 512],
                    start=False, stop=True), incs=[(SP, "sp", 1)])
                wo_done = p.cur("sp")
                w2 = [(SP, wo_done)]
                if dch == 0 and att_dma[par]:
                    w2.append((SATT2[par], att_dma[par]))
                attb_evac[par] = sop(lambda sc, par=par, dch=dch: sc.activation(
                    attb[:, par, dch * 512:(dch + 1) * 512], pw[:, dch, :], AF.Copy),
                    waits=w2)
            att_dma[par] = dma("sync", satt_names[par], SATT2[par],
                               lambda s, tt2=tt2, par=par: s.dma_start(
                                   att_h[tt2 * 128:(tt2 + 1) * 128, :], attb[:, par, :]),
                               waits=[(SA, attb_evac[par])])
        pc_sa, pc_sp, pc_sv = p.cur("sa"), p.cur("sp"), p.cur("sv")

        if "att" in dbg:
            dma("sync", "sy", SY, lambda s: s.dma_start(dbg["att"][:], att_h[:]),
                waits=[(SATT2[0], p.cur("satt")), (SATT2[1], p.cur("satt1"))])

        for g in reversed(pc_guards):
            g.__exit__(None, None, None)

    if stage >= 5:
        for _ in range(5):
            g = ctx_stack.pop()
            g.__exit__(None, None, None)

        SQW = [sems["sqw0"], sems["sqw1"]]
        SFN, SFP = sems["sfn"], sems["sfp"]
        SFX = [sems["sfx0"], sems["sfx1"]]
        SFA = [sems["sfa0"], sems["sfa1"]]
        SYO = [sems["syo0"], sems["syo1"]]

        fe_guards = []

        def sbE(name, shape, dtype):
            g = nc.sbuf_tensor(name, shape, dtype)
            fe_guards.append(g)
            return g.__enter__()

        w1q = sbE("w1q", [128, 2, DC, 1024], BF16)
        w2q = sbE("w2q", [128, 2, DC, 1024], BF16)
        acc = sbE("acc", [128, 8, D], F32)
        nrmhc = sbE("nrmhc", [128, 2, DC, 1024], BF16)
        xeb = sbE("xeb", [128, 2, DC, 512], BF16)
        hTb = sbE("hTb", [128, 2, DC, 512], BF16)
        b2row_s = sbE("b2row_s", [1, NE * D], BF16)
        b1f_s = sbE("b1f_s", [128, NE, HID // 128], F32)
        elns_s = sbE("elns_s", [128, NE, DC], F32)
        elnb_s = sbE("elnb_s", [128, NE, DC], F32)
        xq = sbE("xq", [128, 2, D], F32)
        atq = sbE("atq", [128, 2, D], BF16)

        fence_fe = [(SA, pc_sa), (SV, pc_sv), (SP, pc_sp),
                    (SATT2[0], p.cur("satt")), (SATT2[1], p.cur("satt1"))]

        dma("gpsimd", "sfp", SFP, lambda g: g.dma_start(b2row_s[:], b2f_h[:]),
            waits=fence_fe)
        dma("gpsimd", "sfp", SFP, lambda g: g.dma_start(
            b1f_s[:], b1f_h.rearrange("e p c -> p e c")))
        dma("gpsimd", "sfp", SFP, lambda g: g.dma_start(
            elns_s[:], elns_h.rearrange("e p c -> p e c")))
        ffp_ld = dma("gpsimd", "sfp", SFP, lambda g: g.dma_start(
            elnb_s[:], elnb_h.rearrange("e p c -> p e c")))

        qw_ld = {}
        qw_used = {}
        nq_total = NH * NE * 4

        def emit_qload(qidx):
            hq, rem = divmod(qidx, NE * 4)
            e, jq = divmod(rem, 4)
            slot = qidx % 2
            w = list(fence_fe) if qidx < 2 else [(SP, qw_used[qidx - 2])]
            dma("gpsimd", "sqw%d" % slot, SQW[slot],
                lambda g, e=e, jq=jq, slot=slot: g.dma_start(
                    w1q[:, slot, :, :],
                    w1tf_h[e].rearrange("(c p) j -> p c j", p=128)[
                        :, :, jq * 1024:(jq + 1) * 1024]),
                waits=w)
            qw_ld[qidx] = dma("gpsimd", "sqw%d" % slot, SQW[slot],
                lambda g, e=e, jq=jq, slot=slot: g.dma_start(
                    w2q[:, slot, :, :],
                    w2tf_h[e][jq * 1024:(jq + 1) * 1024, :].rearrange(
                        "(c p) d -> p c d", p=128)))

        emit_qload(0)
        emit_qload(1)

        nh_ld = [0] * NH
        q_last_xeT = [0] * NH

        def emit_nrmhc_load(hq):
            w = list(fence_fe) if hq < 2 else [(SV, q_last_xeT[hq - 2])]
            nh_ld[hq] = dma("scalar", "sfn", SFN,
                            lambda sc_, hq=hq: sc_.dma_start(
                                nrmhc[:, hq % 2, :, :],
                                nrm_h[:, :, hq * QT:(hq + 1) * QT]), waits=w)

        emit_nrmhc_load(0)

        acc_init_done = [0] * 8
        y_stored = [{} for _ in range(2)]
        xslot_reader = [0] * 2
        xa_ld = {}

        def emit_acc_dmas(hq, tls):
            tbase = hq * QT
            for tl in tls:
                px = tl % 2
                pq = tl % 2
                tg = tbase + tl * 128
                w = []
                if hq:
                    w.append((SYO[pq], y_stored[pq][tl]))
                elif tl < 2:
                    w = list(fence_fe)
                if xslot_reader[px]:
                    w.append((SV, xslot_reader[px]))
                xl = dma("sync", "sfx%d" % pq, SFX[pq],
                         lambda s_, tg=tg, px=px: s_.dma_start(
                             xq[:, px, :], x_h[tg:tg + 128, :]), waits=w)
                al = dma("sync", "sfa%d" % pq, SFA[pq],
                         lambda s_, tg=tg, px=px: s_.dma_start(
                             atq[:, px, :], att_h[tg:tg + 128, :]))
                xa_ld[tl] = (pq, xl, al)

        def emit_acc_adds(tls):
            for tl in tls:
                px = tl % 2
                pq, xl, al = xa_ld[tl]
                acc_init_done[tl] = vop(
                    lambda v, tl=tl, px=px: v.tensor_add(
                        acc[:, tl, :], xq[:, px, :], atq[:, px, :]),
                    waits=[(SFX[pq], xl), (SFA[pq], al)])
                xslot_reader[px] = acc_init_done[tl]

    if stage >= 4:
        SRN = [sems["srn0"], sems["srn1"]]
        pd_guards = []

        def sbD(name, shape, dtype):
            g = nc.sbuf_tensor(name, shape, dtype)
            pd_guards.append(g)
            return g.__enter__()

        nrmr = sbD("nrmr", [128, 2, DC, 128], BF16)
        rlnt = sbD("rlnt", [128, 2, DC, 128], BF16)
        lbuf = sbD("lbuf", [128, TT, NE], F32)
        qbuf = sbD("qbuf", [128, TT, NE], F32)
        mbuf = sbD("mbuf", [128, TT, NE], F32)
        dbuf = sbD("dbuf", [128, TT, NE], F32)
        wrk = sbD("wrk", [128, TT, 4], F32)
        pr_g = nc.psum_tensor("pr", [128, 2, 512], F32)
        pd_guards.append(pr_g)
        pr = pr_g.__enter__()

        fence_pc = [(SA, pc_sa), (SV, pc_sv), (SP, pc_sp),
                    (SATT2[0], p.cur("satt")), (SATT2[1], p.cur("satt1"))]
        rln_done = [0, 0]
        rmm_done = [0, 0]
        lcp_done = {}

        def emit_rln(t):
            sl2 = t % 2
            w = list(fence_pc) if t < 2 else [(SV, rln_done[sl2])]
            ld = dma("sync", "srn%d" % sl2, SRN[sl2],
                     lambda s, t=t, sl2=sl2: s.dma_start(
                         nrmr[:, sl2, :, :],
                         nrm_h[:, :, t * 128:(t + 1) * 128]),
                     waits=w)
            w = [(SRN[sl2], ld)]
            if rmm_done[sl2]:
                w.append((SP, rmm_done[sl2]))
            for kc in range(DC):
                rln_done[sl2] = vop(lambda v, sl2=sl2, kc=kc: v.tensor_scalar(
                    rlnt[:, sl2, kc, :], nrmr[:, sl2, kc, :],
                    rlns_s[:, kc:kc + 1], rlnb_s[:, kc:kc + 1],
                    ALU.mult, ALU.add),
                    waits=(w if kc == 0 else ()))

        def emit_lmm(t):
            sl2 = t % 2
            bk = t % 2
            w = [(SV, rln_done[sl2]), (SW, pers_loaded)]
            if lcp_done.get(t - 2) is not None:
                w.append((SV, lcp_done[t - 2]))
            if t < 2:
                w += [(SA, pc_sa), (SV, pc_sv)]
            for kc in range(DC):
                p.op("tensor", lambda t_, bk=bk, sl2=sl2, kc=kc: t_.matmul(
                    pr[:, bk, 0:8], rlnt[:, sl2, kc, :], rwt_s[:, kc, :],
                    start=(kc == 0), stop=False),
                    waits=(w if kc == 0 else ()))
            p.op("tensor", lambda t_, bk=bk: t_.matmul(
                pr[:, bk, 0:8], ones_row[:], rb_s[0:1, :],
                start=False, stop=True), incs=[(SP, "sp", 1)])
            rmm_done[sl2] = p.cur("sp")
            lcp_done[t] = vop(lambda v, t=t, bk=bk: v.tensor_copy(
                lbuf[:, t, :], pr[:, bk, 0:8]), waits=[(SP, rmm_done[sl2])])

        emit_rln(0)
        for t in range(TT):
            if t + 1 < TT:
                emit_rln(t + 1)
            emit_lmm(t)

        vop(lambda v: v.reduce_max(wrk[:, :, 0:1], lbuf[:], axis=AX.X))
        sub_done = vop(lambda v: v.tensor_sub(
            mbuf[:], lbuf[:], wrk[:, :, 0:1].broadcast_to([128, TT, NE])))
        exp_done = sop(lambda sc: sc.activation(qbuf[:], mbuf[:], AF.Exp),
                       waits=[(SV, sub_done)])
        vop(lambda v: v.tensor_scalar(
            mbuf[:], qbuf[:], 1.0, -1e30, ALU.is_ge, ALU.mult),
            waits=[(SA, exp_done)])
        vop(lambda v: v.tensor_add(mbuf[:], qbuf[:], mbuf[:]))
        vop(lambda v: v.reduce_max(wrk[:, :, 1:2], mbuf[:], axis=AX.X))
        vop(lambda v: v.tensor_sub(
            dbuf[:], qbuf[:], wrk[:, :, 1:2].broadcast_to([128, TT, NE])))
        vop(lambda v: v.tensor_scalar(
            dbuf[:], dbuf[:], 0.0, None, ALU.is_ge))
        vop(lambda v: v.tensor_mul(qbuf[:], qbuf[:], dbuf[:]))
        vop(lambda v: v.reduce_sum(wrk[:, :, 2:3], qbuf[:], axis=AX.X))
        vop(lambda v: v.reciprocal(wrk[:, :, 3:4], wrk[:, :, 2:3]))
        vop(lambda v: v.tensor_mul(
            gates[:], qbuf[:], wrk[:, :, 3:4].broadcast_to([128, TT, NE])))

        pd_sa, pd_sv, pd_sp = p.cur("sa"), p.cur("sv"), p.cur("sp")

        if "gates" in dbg:
            dma("sync", "sy", SY, lambda s: s.dma_start(dbg["gates"][:], gates[:]),
                waits=[(SV, pd_sv)])

        for g in reversed(pd_guards):
            g.__exit__(None, None, None)

    if stage >= 5:
        pm1_g = nc.psum_tensor("pm1", [128, 2, 512], F32)
        fe_guards.append(pm1_g); pm1 = pm1_g.__enter__()
        pys = []
        for _i in range(6):
            _g = nc.psum_tensor("py%d" % _i, [128, 512], F32)
            fe_guards.append(_g); pys.append(_g.__enter__())

        xeT_done = {}
        xeb_used_k = {}
        pm1_evac = [0, 0]
        py_evac = {}
        hT_used = [0, 0]
        acc_evac = {}

        iters = [(hq, e, jq, tb)
                 for hq in range(NH) for e in range(NE)
                 for jq in range(4) for tb in range(2)]
        NIT = len(iters)

        def emit_xeT(k):
            hq, e, jq, tb = iters[k]
            xsl = k % 2
            w = [(SFN, nh_ld[hq]), (SFP, ffp_ld)]
            if k >= 2 and xeb_used_k.get(k - 2):
                w.append((SP, xeb_used_k[k - 2]))
            for kc in range(DC):
                vop(lambda v, hq=hq, e=e, kc=kc, tb=tb, xsl=xsl: v.tensor_scalar(
                    xeb[:, xsl, kc, :],
                    nrmhc[:, hq % 2, kc, tb * 512:(tb + 1) * 512],
                    elns_s[:, e, kc:kc + 1], elnb_s[:, e, kc:kc + 1],
                    ALU.mult, ALU.add),
                    waits=(w if kc == 0 else ()))
            xeT_done[k] = p.cur("sv")
            q_last_xeT[hq] = xeT_done[k]

        emit_nrmhc_load(1)
        emit_xeT(0)
        emit_acc_dmas(0, range(2))
        emit_acc_adds(range(2))
        for _tl in range(2, 8):
            emit_acc_dmas(0, [_tl])
            emit_acc_adds([_tl])

        for k in range(NIT):
            hq, e, jq, tb = iters[k]
            qidx = (hq * NE + e) * 4 + jq
            slot = qidx % 2
            xsl = k % 2
            if k + 1 < NIT:
                emit_xeT(k + 1)
            for js in range(DC):
                sl1 = js % 2
                w = [(SV, xeT_done[k]), (SQW[slot], qw_ld[qidx])]
                if pm1_evac[sl1]:
                    w.append((SA, pm1_evac[sl1]))
                for kc in range(DC):
                    p.op("tensor",
                         lambda t, sl1=sl1, js=js, kc=kc, slot=slot, xsl=xsl:
                         t.matmul(
                             pm1[:, sl1, :],
                             w1q[:, slot, kc, js * 128:(js + 1) * 128],
                             xeb[:, xsl, kc, :],
                             start=(kc == 0), stop=(kc == DC - 1)),
                         waits=(w if kc == 0 else ()),
                         incs=([(SP, "sp", 1)] if kc == DC - 1 else ()))
                m1_done = p.cur("sp")
                gw = [(SP, m1_done)]
                if js == 0 and hT_used[tb]:
                    gw.append((SP, hT_used[tb]))
                pm1_evac[sl1] = sop(
                    lambda sc, sl1=sl1, js=js, tb=tb, e=e, jq=jq:
                    sc.activation(
                        hTb[:, tb, js, :], pm1[:, sl1, :], AF.Gelu,
                        bias=b1f_s[:, e, jq * 8 + js:jq * 8 + js + 1]),
                    waits=gw)
            xeb_used_k[k] = p.cur("sp")
            h_done = p.cur("sa")
            for gi in range(4):
                dch, tp = divmod(gi, 2)
                bsel = (gi % 3) * 2
                w = [(SA, h_done)]
                if py_evac.get(bsel):
                    w.append((SV, py_evac[bsel]))
                first = True
                for js in range(DC):
                    for ts in range(2):
                        tsub = tp * 2 + ts
                        last = (js == DC - 1 and ts == 1 and jq != 3)
                        p.op("tensor",
                             lambda t, js=js, tsub=tsub, dch=dch,
                             bsel=bsel, ts=ts, slot=slot, tb=tb, jq=jq:
                             t.matmul(
                                 pys[bsel + ts][:],
                                 hTb[:, tb, js,
                                     tsub * 128:(tsub + 1) * 128],
                                 w2q[:, slot, js,
                                     dch * 512:(dch + 1) * 512],
                                 start=(js == 0),
                                 stop=(js == DC - 1 and jq != 3)),
                             waits=(w if first else ()),
                             incs=([(SP, "sp", 1)] if last else ()))
                        first = False
                if jq == 3:
                    for ts in range(2):
                        p.op("tensor", lambda t, bsel=bsel, ts=ts,
                             e=e, dch=dch: t.matmul(
                                 pys[bsel + ts][:], ones_row[:],
                                 b2row_s[0:1, e * D + dch * 512:
                                         e * D + (dch + 1) * 512],
                                 start=False, stop=True),
                             incs=([(SP, "sp", 1)] if ts == 1 else ()))
                g_done = p.cur("sp")
                if gi == 3:
                    hT_used[tb] = g_done
                for ts in range(2):
                    tsub = tp * 2 + ts
                    tl = tb * 4 + tsub
                    ttg = hq * 8 + tl
                    w2 = [(SP, g_done)]
                    if e == 0 and jq == 0:
                        w2.append((SV, acc_init_done[tl]))
                    py_evac[bsel] = vop(
                        lambda v, bsel=bsel, ts=ts, tl=tl, ttg=ttg,
                        dch=dch, e=e: v.scalar_tensor_tensor(
                            acc[:, tl, dch * 512:(dch + 1) * 512],
                            pys[bsel + ts][:],
                            gates[:, ttg, e:e + 1],
                            acc[:, tl, dch * 512:(dch + 1) * 512],
                            ALU.mult, ALU.add),
                        waits=w2)
                    if e == NE - 1 and jq == 3:
                        acc_evac[(tl, dch)] = py_evac[bsel]
            qw_used[qidx] = p.cur("sp")
            if tb == 1 and qidx + 2 < nq_total:
                emit_qload(qidx + 2)
            if tb == 1 and jq == 3 and e == NE - 1:
                tbase = hq * QT
                for tl in range(8):
                    par = tl % 2
                    tg = tbase + tl * 128
                    y_stored[par][tl] = dma(
                        "sync", "syo%d" % par, SYO[par],
                        lambda sy, tg=tg, tl=tl: sy.dma_start(
                            y_h[tg:tg + 128, :], acc[:, tl, :]),
                        waits=[(SV, acc_evac[(tl, 0)]), (SV, acc_evac[(tl, 1)])])
                if hq + 1 < NH:
                    for _tl in range(8):
                        emit_acc_dmas(hq + 1, [_tl])
                        emit_acc_adds([_tl])
                if hq + 2 < NH:
                    emit_nrmhc_load(hq + 2)

        for g in reversed(fe_guards):
            g.__exit__(None, None, None)

    if stage < 5:
        dma("gpsimd", "sdum", sems["sdum"], lambda g: g.dma_start(
            y_h[0:128, 0:128], ident[:]),
            waits=[(SA, p.cur("sa")), (SV, p.cur("sv"))])

    for nm, sem in sems.items():
        if nm in ("sa", "sv", "sp") or p.cur(nm) == 0:
            continue
        p.wait("sync", sem, p.cur(nm))

    with nc.Block() as block:
        block.tensor(lambda t: [f(t) for f in p.ops["tensor"]])
        block.scalar(lambda sc: [f(sc) for f in p.ops["scalar"]])
        block.vector(lambda v: [f(v) for f in p.ops["vector"]])
        block.sync(lambda s: [f(s) for f in p.ops["sync"]])
        block.gpsimd(lambda g: [f(g) for f in p.ops["gpsimd"]])

    for g in reversed(p0_guards):
        g.__exit__(None, None, None)
    for g in reversed(ctx_stack):
        g.__exit__(None, None, None)
    return nc


def _prep_host(inputs):
    A = np.ascontiguousarray
    com = {
        "w1t_a": A(inputs["L0_attn_W1"].T),
        "b1_a": A(np.reshape(inputs["L0_attn_b1"], (DC, 128)).T),
        "w2t_a": A(inputs["L0_attn_W2"].T),
        "b2_a": A(np.reshape(inputs["L0_attn_b2"], (DC, 128)).T),
        "wgt": A(inputs["L0_attn_Wg"].T),
        "bg_r": A(np.reshape(inputs["L0_attn_bg"], (24, 128)).T),
        "wot": A(inputs["L0_attn_Wo"].T),
        "bo_row": A(np.reshape(inputs["L0_attn_bo"], (1, D))),
        "rlns": A(np.reshape(inputs["L0_ffn_router_ln_scale"], (DC, 128)).T),
        "rlnb": A(np.reshape(inputs["L0_ffn_router_ln_bias"], (DC, 128)).T),
        "rwt": A(inputs["L0_ffn_router_w"].T),
        "rb_row": A(np.reshape(inputs["L0_ffn_router_b"], (1, NE))),
        "elns": A(np.reshape(inputs["L0_ffn_expert_ln_scale"], (NE, DC, 128)).transpose(0, 2, 1)),
        "elnb": A(np.reshape(inputs["L0_ffn_expert_ln_bias"], (NE, DC, 128)).transpose(0, 2, 1)),
        "w1t_f": A(inputs["L0_ffn_w1"].transpose(0, 2, 1)),
        "b1_f": A(np.reshape(inputs["L0_ffn_b1"], (NE, HID // 128, 128)).transpose(0, 2, 1)),
        "w2t_f": A(inputs["L0_ffn_w2"].transpose(0, 2, 1)),
        "b2_row": A(np.reshape(inputs["L0_ffn_b2"], (1, NE * D))),
    }
    xs = np.asarray(inputs["L0_attn_x"])
    return [dict(com, x=A(xs[b])) for b in range(8)]


def kernel(**inputs):
    in_maps = _prep_host({k: np.asarray(v) for k, v in inputs.items()})
    nc = build_nc()
    res = run_bass_kernel_spmd(nc, in_maps, core_ids=list(range(8)))
    out = np.stack([np.asarray(r["y"]) for r in res.results], axis=0)
    return out.astype(np.float32)


# version 18
# speedup vs baseline: 1.0636x; 1.0636x over previous
import numpy as np
import ml_dtypes
import concourse.bass as bass
import concourse.mybir as mybir
from concourse.bass_utils import run_bass_kernel_spmd

F32 = mybir.dt.float32
BF16 = mybir.dt.bfloat16
AF = mybir.ActivationFunctionType
ALU = mybir.AluOpType
AX = mybir.AxisListType

D = 1024
T = 4096
HD = 64
NE = 8
HID = 4096
DC = D // 128
TT = T // 128
NPAIR = 8
IB = T // 512
NH = 4
QT = T // NH
BIGNEG = -1e30

ENGINES = ("tensor", "scalar", "vector", "sync", "gpsimd")
STAGE = 5
DEBUG = ()


class Plan:
    def __init__(self):
        self.ops = {e: [] for e in ENGINES}
        self.cnt = {}

    def bump(self, name, n=1):
        self.cnt[name] = self.cnt.get(name, 0) + n
        return self.cnt[name]

    def cur(self, name):
        return self.cnt.get(name, 0)

    def op(self, eng, build, waits=(), incs=()):
        planned = []
        for sem, name, n in incs:
            self.bump(name, n)
            planned.append((sem, n))
        waits = tuple((s, v) for (s, v) in waits if v > 0)

        def f(E, build=build, waits=waits, pincs=tuple(planned)):
            for s, v in waits:
                E.wait_ge(s, v)
            ins = build(E)
            for s, n in pincs:
                ins = ins.then_inc(s, n)

        self.ops[eng].append(f)

    def wait(self, eng, sem, val):
        if val > 0:
            self.ops[eng].append(lambda E, s=sem, v=val: E.wait_ge(s, v))


def _mask_tile(o):
    istart = 0 if o == 0 else 128 * o - 64
    N = 512 if o == 0 else 576 - 128 * o
    jl = np.arange(128)[:, None]
    ii = np.arange(N)[None, :]
    allowed = (128 * o + jl - istart - ii) <= 63
    return allowed.astype(np.float32).astype(ml_dtypes.bfloat16)


def build_nc(stage=None, debug=None):
    stage = STAGE if stage is None else stage
    debug = DEBUG if debug is None else debug
    nc = bass.Bass("TRN2", target_bir_lowering=False)
    p = Plan()
    ctx_stack = []

    def keep(g):
        ctx_stack.append(g)
        return g.__enter__()

    x_h = nc.dram_tensor("x", [T, D], F32, kind="ExternalInput").ap()
    w1ta_h = nc.dram_tensor("w1t_a", [D, D], F32, kind="ExternalInput").ap()
    b1a_h = nc.dram_tensor("b1_a", [128, DC], F32, kind="ExternalInput").ap()
    w2ta_h = nc.dram_tensor("w2t_a", [D, D], F32, kind="ExternalInput").ap()
    b2a_h = nc.dram_tensor("b2_a", [128, DC], F32, kind="ExternalInput").ap()
    wgt_h = nc.dram_tensor("wgt", [D, 3 * D], F32, kind="ExternalInput").ap()
    bgr_h = nc.dram_tensor("bg_r", [128, 24], F32, kind="ExternalInput").ap()
    wot_h = nc.dram_tensor("wot", [D, D], F32, kind="ExternalInput").ap()
    bo_h = nc.dram_tensor("bo_row", [1, D], F32, kind="ExternalInput").ap()
    rlns_h = nc.dram_tensor("rlns", [128, DC], F32, kind="ExternalInput").ap()
    rlnb_h = nc.dram_tensor("rlnb", [128, DC], F32, kind="ExternalInput").ap()
    rwt_h = nc.dram_tensor("rwt", [D, NE], F32, kind="ExternalInput").ap()
    rb_h = nc.dram_tensor("rb_row", [1, NE], F32, kind="ExternalInput").ap()
    elns_h = nc.dram_tensor("elns", [NE, 128, DC], F32, kind="ExternalInput").ap()
    elnb_h = nc.dram_tensor("elnb", [NE, 128, DC], F32, kind="ExternalInput").ap()
    w1tf_h = nc.dram_tensor("w1t_f", [NE, D, HID], F32, kind="ExternalInput").ap()
    b1f_h = nc.dram_tensor("b1_f", [NE, 128, HID // 128], F32, kind="ExternalInput").ap()
    w2tf_h = nc.dram_tensor("w2t_f", [NE, HID, D], F32, kind="ExternalInput").ap()
    b2f_h = nc.dram_tensor("b2_row", [1, NE * D], F32, kind="ExternalInput").ap()

    y_h = nc.dram_tensor("y", [T, D], F32, kind="ExternalOutput").ap()
    nrm_h = nc.dram_tensor("nrm_hbm", [128, DC, T], BF16, kind="Internal").ap()
    xnT_h = nc.dram_tensor("xnT_hbm", [128, DC, T], BF16, kind="Internal").ap()
    att_h = nc.dram_tensor("att_hbm", [T, D], BF16, kind="Internal").ap()

    ident_h = nc.inline_tensor(
        np.eye(128, dtype=np.float32).astype(ml_dtypes.bfloat16), "identc").ap()
    masks_h = [nc.inline_tensor(_mask_tile(o), f"maskc{o}").ap() for o in range(5)]

    dbg = {}
    def dbgten(name, shape, dtype):
        if name in debug:
            dbg[name] = nc.dram_tensor("dbg_" + name, shape, dtype, kind="ExternalOutput").ap()
    dbgten("xnT", [128, DC, T], BF16)
    dbgten("nrm", [128, DC, T], BF16)
    dbgten("a2T", [128, DC, T], BF16)
    dbgten("qkT", [128, 2, T], BF16)
    dbgten("vaug", [128, TT, 130], BF16)
    dbgten("unorm", [128, NPAIR, T], BF16)
    dbgten("att", [T, D], BF16)
    dbgten("gates", [128, TT, NE], F32)
    dbgten("p0s", [128, 2, 16], F32)
    dbgten("p0xb", [128, 2, D], F32)
    dbgten("p0xnf", [128, D], F32)
    dbgten("p0sqb", [128, D], F32)
    dbgten("p0xnbf", [128, 2, D], BF16)
    dbgten("xnbf0", [128, D], BF16)

    sems = {}
    for n in ["sx0", "sx1", "sw", "swa", "sa", "sv", "sp", "snrm0", "snrm1",
              "satt", "satt1", "swg", "swo", "sy", "sdum", "srn0", "srn1",
              "sxnt0", "sxnt1", "sxn0", "sxn1",
              "sqw0", "sqw1", "sfn", "sfp", "sfx0", "sfx1", "sfa0", "sfa1",
              "syo0", "syo1"]:
        sems[n] = keep(nc.semaphore(n))
    SX = [sems["sx0"], sems["sx1"]]
    SW, SA, SV, SP = sems["sw"], sems["sa"], sems["sv"], sems["sp"]
    SWA = sems["swa"]
    SNRM = [sems["snrm0"], sems["snrm1"]]
    SY = sems["sy"]

    def vop(build, waits=()):
        p.op("vector", build, waits=tuple(waits) + ((SV, p.cur("sv")),),
             incs=[(SV, "sv", 1)])
        return p.cur("sv")

    def sop(build, waits=()):
        p.op("scalar", build, waits=tuple(waits) + ((SA, p.cur("sa")),),
             incs=[(SA, "sa", 1)])
        return p.cur("sa")

    def dma(eng, semname, sem, build, waits=()):
        w = tuple(waits) + ((sem, p.cur(semname)),)
        p.op(eng, build, waits=w, incs=[(sem, semname, 16)])
        return p.cur(semname)

    ident = keep(nc.sbuf_tensor("ident", [128, 128], BF16))
    b1a = keep(nc.sbuf_tensor("b1a", [128, DC], F32))
    b2a = keep(nc.sbuf_tensor("b2a", [128, DC], F32))
    bgr = keep(nc.sbuf_tensor("bgr", [128, 24], F32))
    bgr_bf = keep(nc.sbuf_tensor("bgr_bf", [128, 8], BF16))
    ones_row = keep(nc.sbuf_tensor("ones_row", [1, 128], BF16))
    rlns_s = keep(nc.sbuf_tensor("rlns_s", [128, DC], F32))
    rlnb_s = keep(nc.sbuf_tensor("rlnb_s", [128, DC], F32))
    rwt_s = keep(nc.sbuf_tensor("rwt_s", [128, DC, NE], BF16))
    rb_s = keep(nc.sbuf_tensor("rb_s", [1, NE], BF16))
    gates = keep(nc.sbuf_tensor("gates", [128, TT, NE], F32))

    for fn in [
        lambda g: g.dma_start(ident[:], ident_h[:]),
        lambda g: g.dma_start(b1a[:], b1a_h[:]),
        lambda g: g.dma_start(b2a[:], b2a_h[:]),
        lambda g: g.dma_start(bgr[:], bgr_h[:]),
        lambda g: g.dma_start(bgr_bf[:], bgr_h[:, 16:24]),
        lambda g: g.dma_start(rlns_s[:], rlns_h[:]),
        lambda g: g.dma_start(rlnb_s[:], rlnb_h[:]),
        lambda g: g.dma_start(rwt_s[:], rwt_h.rearrange("(c p) e -> p c e", p=128)),
        lambda g: g.dma_start(rb_s[:], rb_h[:]),
    ]:
        dma("gpsimd", "sw", SW, fn)
    pers_loaded = p.cur("sw")
    p.op("vector", lambda v: v.memset(ones_row[:], 1.0), incs=[(SV, "sv", 1)])
    ones_ready = p.cur("sv")

    a2T = keep(nc.sbuf_tensor("a2T", [128, DC, T], BF16))
    w1ta = keep(nc.sbuf_tensor("w1ta", [128, DC, D], BF16))
    w2ta = keep(nc.sbuf_tensor("w2ta", [128, DC, D], BF16))
    a1bf = keep(nc.sbuf_tensor("a1bf", [128, DC, 512], BF16))
    xnst = keep(nc.sbuf_tensor("xnst", [128, 2, DC, 512], BF16))

    dma("gpsimd", "swa", SWA, lambda g: g.dma_start(
        w1ta[:], w1ta_h.rearrange("(c p) j -> p c j", p=128)))
    pa_w_loaded = dma("gpsimd", "swa", SWA, lambda g: g.dma_start(
        w2ta[:], w2ta_h.rearrange("(c p) j -> p c j", p=128)))

    p0_guards = []

    def sb0(name, shape, dtype):
        g = nc.sbuf_tensor(name, shape, dtype)
        p0_guards.append(g)
        return g.__enter__()

    xb4 = sb0("xb4", [128, 4, D], F32)
    xcf4 = sb0("xcf4", [128, 4, D], F32)
    sqb4 = sb0("sqb4", [128, 4, D], F32)
    xnbf4 = sb0("xnbf4", [128, 4, D], BF16)
    nrmbf4 = sb0("nrmbf4", [128, 4, D], BF16)
    nrmc = sb0("nrmc", [128, 2, DC, 128], BF16)
    st4 = sb0("st4", [128, 4, 12], F32)

    p0_psum_guards = []
    g = nc.psum_tensor("pt_xn0", [128, DC, 128], BF16)
    p0_psum_guards.append(g); pt_xn = [g.__enter__()]
    g = nc.psum_tensor("pt_nrm0", [128, DC, 128], BF16)
    p0_psum_guards.append(g); pt_nrm = [g.__enter__()]

    pa_guards = []
    g = nc.psum_tensor("pa1", [128, 2, 512], F32); pa_guards.append(g)
    pa1 = g.__enter__()
    g = nc.psum_tensor("pa2", [128, 2, 2, 512], F32); pa_guards.append(g)
    pa2 = g.__enter__()

    nrm_evac_done = [0] * TT
    nrm_dma_done = [0] * TT
    xn_evac_done = [0] * TT
    prev_xc2 = 0
    prev_rstd2 = 0
    prev_tr = 0
    a1_evac = [0, 0]
    pa2_evac = [0, 0]
    xnst_used = [0, 0]

    for bb in range(8):
        w = []
        if bb >= 1:
            w.append((SV, prev_xc2))
        p.op("sync", lambda sy, bb=bb: sy.dma_start(
            xb4[:], x_h[bb * 512:(bb + 1) * 512, :].rearrange(
                "(t p) d -> p t d", p=128)), waits=w, incs=[(SX[0], "sx0", 16)])
        x_loaded = p.cur("sx0")

        w = [(SX[0], x_loaded)]
        if bb >= 1:
            w.append((SA, prev_rstd2))
        vop(lambda v: v.reduce_sum(st4[:, :, 0:1], xb4[:], axis=AX.X), waits=w)
        vop(lambda v: v.tensor_scalar_mul(st4[:, :, 1:2], st4[:, :, 0:1], 1.0 / D))
        for i in range(4):
            vop(lambda v, i=i: v.tensor_scalar(
                xcf4[:, i, :], xb4[:, i, :], st4[:, i, 1:2], None, ALU.subtract))
        xc_done = p.cur("sv")
        sq_done = sop(lambda sc: sc.activation(sqb4[:], xcf4[:], AF.Square),
                      waits=[(SV, xc_done)])
        vop(lambda v: v.reduce_sum(st4[:, :, 2:3], sqb4[:], axis=AX.X),
            waits=[(SA, sq_done)])
        var_done = vop(lambda v: v.tensor_scalar(
            st4[:, :, 3:4], st4[:, :, 2:3], 1.0 / D, 1e-5, ALU.mult, ALU.add))
        sop(lambda sc: sc.activation(st4[:, :, 4:5], st4[:, :, 3:4], AF.Ln),
            waits=[(SV, var_done)])
        rstd_done = sop(lambda sc: sc.activation(
            st4[:, :, 5:6], st4[:, :, 4:5], AF.Exp, scale=-0.5))
        for i in range(4):
            vop(lambda v, i=i: v.tensor_scalar(
                xb4[:, i, :], xcf4[:, i, :], st4[:, i, 5:6], None, ALU.mult),
                waits=([(SA, rstd_done)] if i == 0 else ()))
        w = []
        if bb >= 1:
            w.append((SP, prev_tr))
        xnbf_done = vop(lambda v: v.tensor_copy(xnbf4[:], xb4[:]), waits=w)
        vop(lambda v: v.reduce_sum(st4[:, :, 6:7], xb4[:], axis=AX.X))
        vop(lambda v: v.tensor_scalar_mul(st4[:, :, 7:8], st4[:, :, 6:7], 1.0 / D))
        for i in range(4):
            vop(lambda v, i=i: v.tensor_scalar(
                xcf4[:, i, :], xb4[:, i, :], st4[:, i, 7:8], None, ALU.subtract))
        xc2_done = p.cur("sv")
        prev_xc2 = xc2_done
        sq2_done = sop(lambda sc: sc.activation(sqb4[:], xcf4[:], AF.Square),
                       waits=[(SV, xc2_done)])
        vop(lambda v: v.reduce_sum(st4[:, :, 8:9], sqb4[:], axis=AX.X),
            waits=[(SA, sq2_done)])
        var2_done = vop(lambda v: v.tensor_scalar(
            st4[:, :, 9:10], st4[:, :, 8:9], 1.0 / D, 1e-6, ALU.mult, ALU.add))
        sop(lambda sc: sc.activation(st4[:, :, 10:11], st4[:, :, 9:10], AF.Ln),
            waits=[(SV, var2_done)])
        rstd2_done = sop(lambda sc: sc.activation(
            st4[:, :, 11:12], st4[:, :, 10:11], AF.Exp, scale=-0.5))
        prev_rstd2 = rstd2_done
        for i in range(4):
            vop(lambda v, i=i: v.tensor_scalar(
                nrmbf4[:, i, :], xcf4[:, i, :], st4[:, i, 11:12], None, ALU.mult),
                waits=([(SA, rstd2_done)] if i == 0 else ()))
        nrm_done = p.cur("sv")

        spar = bb % 2
        for i in range(4):
            tt = bb * 4 + i
            par = tt % 2
            w = [(SV, xnbf_done)]
            if tt == 0:
                w.append((SW, pers_loaded))
            if tt >= 1:
                w.append((SV, xn_evac_done[tt - 1]))
            if i == 0 and xnst_used[spar]:
                w.append((SP, xnst_used[spar]))
            for c in range(DC):
                p.op("tensor", lambda t, c=c, i=i: t.transpose(
                    pt_xn[0][:, c, :], xnbf4[:, i, c * 128:(c + 1) * 128],
                    ident[:]),
                    waits=(w if c == 0 else ()),
                    incs=([(SP, "sp", 1)] if c == DC - 1 else ()))
            trxn_done = p.cur("sp")
            w = [(SV, nrm_done)]
            if tt >= 1:
                w.append((SA, nrm_evac_done[tt - 1]))
            for c in range(DC):
                p.op("tensor", lambda t, c=c, i=i: t.transpose(
                    pt_nrm[0][:, c, :], nrmbf4[:, i, c * 128:(c + 1) * 128],
                    ident[:]),
                    waits=(w if c == 0 else ()),
                    incs=([(SP, "sp", 1)] if c == DC - 1 else ()))
            trnrm_done = p.cur("sp")
            xn_evac_done[tt] = vop(lambda v, spar=spar, i=i: v.tensor_copy(
                xnst[:, spar, :, i * 128:(i + 1) * 128], pt_xn[0][:]),
                waits=[(SP, trxn_done)])
            w = [(SP, trnrm_done)]
            if tt >= 2:
                w.append((SNRM[par], nrm_dma_done[tt - 2]))
            nrm_evac_done[tt] = sop(lambda sc, par=par: sc.activation(
                nrmc[:, par, :, :], pt_nrm[0][:], AF.Copy), waits=w)
            nrm_dma_done[tt] = dma("gpsimd", "snrm%d" % par, SNRM[par],
                lambda gy, tt=tt, par=par: gy.dma_start(
                    nrm_h[:, :, tt * 128:(tt + 1) * 128], nrmc[:, par, :, :]),
                waits=[(SA, nrm_evac_done[tt])])
        prev_tr = p.cur("sp")
        xn_batch_done = p.cur("sv")

        t0 = bb * 512
        for jc in range(DC):
            sl = jc % 2
            w = [(SV, xn_batch_done)]
            if bb == 0 and jc == 0:
                w.append((SWA, pa_w_loaded))
            if a1_evac[sl]:
                w.append((SA, a1_evac[sl]))
            for kc in range(DC):
                p.op("tensor", lambda t, sl=sl, jc=jc, kc=kc, spar=spar: t.matmul(
                    pa1[:, sl, :], w1ta[:, kc, jc * 128:(jc + 1) * 128],
                    xnst[:, spar, kc, :],
                    start=(kc == 0), stop=(kc == DC - 1)),
                    waits=(w if kc == 0 else ()),
                    incs=([(SP, "sp", 1)] if kc == DC - 1 else ()))
            mm1_done = p.cur("sp")
            a1_evac[sl] = sop(lambda sc, sl=sl, jc=jc: sc.activation(
                a1bf[:, jc, :], pa1[:, sl, :], AF.Silu, bias=b1a[:, jc:jc + 1]),
                waits=[(SP, mm1_done)])
        xnst_used[spar] = p.cur("sp")
        a1_ready = p.cur("sa")
        for q in range(4):
            pp = q % 2
            w = [(SA, a1_ready)]
            if pa2_evac[pp]:
                w.append((SA, pa2_evac[pp]))
            first = True
            for jc in range(DC):
                for d2 in range(2):
                    dc = 2 * q + d2
                    p.op("tensor", lambda t, pp=pp, d2=d2, dc=dc, jc=jc: t.matmul(
                        pa2[:, pp, d2, :], w2ta[:, jc, dc * 128:(dc + 1) * 128],
                        a1bf[:, jc, :],
                        start=(jc == 0), stop=(jc == DC - 1)),
                        waits=(w if first else ()),
                        incs=([(SP, "sp", 1)] if (jc == DC - 1 and d2 == 1) else ()))
                    first = False
            mm2_done = p.cur("sp")
            for d2 in range(2):
                dc = 2 * q + d2
                p.op("scalar", lambda sc, pp=pp, d2=d2, dc=dc, t0=t0: sc.activation(
                    a2T[:, dc, t0:t0 + 512], pa2[:, pp, d2, :], AF.Identity,
                    bias=b2a[:, dc:dc + 1]),
                    waits=([(SP, mm2_done)] if d2 == 0 else ()),
                    incs=([(SA, "sa", 1)] if d2 == 1 else ()))
            pa2_evac[pp] = p.cur("sa")

    p0_sa, p0_sv, p0_sp = p.cur("sa"), p.cur("sv"), p.cur("sp")
    pa_sa, pa_sp = p0_sa, p0_sp
    pb_early = [(SA, prev_rstd2), (SV, p0_sv), (SX[0], p.cur("sx0"))]

    if "nrm" in dbg:
        dma("sync", "sy", SY, lambda s: s.dma_start(dbg["nrm"][:], nrm_h[:]),
            waits=[(SNRM[0], p.cur("snrm0")), (SNRM[1], p.cur("snrm1"))])

    for g in reversed(pa_guards):
        g.__exit__(None, None, None)
    for g in reversed(p0_psum_guards):
        g.__exit__(None, None, None)

    if "a2T" in dbg:
        dma("sync", "sy", SY, lambda s: s.dma_start(dbg["a2T"][:], a2T[:]),
            waits=[(SA, pa_sa)])

    if stage >= 3:
        for g in reversed(p0_guards):
            g.__exit__(None, None, None)
        p0_guards.clear()
        for _ in range(4):
            g = ctx_stack.pop()
            g.__exit__(None, None, None)

        unorm = keep(nc.sbuf_tensor("unorm", [128, NPAIR, T], BF16))
        wot_sb = keep(nc.sbuf_tensor("wot_sb", [128, DC, D], BF16))
        cvec2 = keep(nc.sbuf_tensor("cvec2", [1, D], BF16))
        bo_sb = keep(nc.sbuf_tensor("bo_sb", [1, D], F32))

        pb_guards = []

        def sbB(name, shape, dtype):
            g = nc.sbuf_tensor(name, shape, dtype)
            pb_guards.append(g)
            return g.__enter__()

        masks_s = sbB("masks_s", [128, 5, 512], BF16)
        wgs = sbB("wgs", [128, 3, DC, 128], BF16)
        qTb = sbB("qTb", [128, T], BF16)
        kTb = sbB("kTb", [128, T], BF16)
        vaug = sbB("vaug", [128, TT, 130], BF16)
        pbuf = sbB("pbuf", [128, 2, 3, 512], BF16)
        usbf = sbB("usbf", [128, 2, 512], F32)
        recd = sbB("recd", [1, 2, 512], BF16)
        lnrow = sbB("lnrow", [1, 2, 512], F32)
        zrow = sbB("zrow", [1, 512], BF16)

        pS_g = nc.psum_tensor("pS", [128, 6, 512], F32); pb_guards.append(pS_g)
        pS = pS_g.__enter__()
        pU0_g = nc.psum_tensor("pU0", [128, 512], F32); pb_guards.append(pU0_g)
        pU1_g = nc.psum_tensor("pU1", [128, 512], F32); pb_guards.append(pU1_g)
        pUs = [pU0_g.__enter__(), pU1_g.__enter__()]

        SWG, SWO = sems["swg"], sems["swo"]
        fence_pa = [(SA, pa_sa), (SP, pa_sp), (SV, p0_sv),
                    (SX[0], p.cur("sx0")), (SX[1], p.cur("sx1")),
                    (SNRM[0], p.cur("snrm0")), (SNRM[1], p.cur("snrm1"))]
        mld = None
        for o in range(5):
            N = 512 if o == 0 else 576 - 128 * o
            mld = dma("gpsimd", "swg", SWG,
                      lambda g, o=o, N=N: g.dma_start(masks_s[:, o, 0:N], masks_h[o][:]),
                      waits=(pb_early if o == 0 else ()))
        masks_ld = mld
        dma("gpsimd", "swo", SWO, lambda g: g.dma_start(
            wot_sb[:], wot_h.rearrange("(c p) d -> p c d", p=128)),
            waits=pb_early)
        wot_ld = dma("gpsimd", "swo", SWO, lambda g: g.dma_start(bo_sb[:], bo_h[:]))

        vop(lambda v: v.memset(zrow[:], 0.0), waits=fence_pa)
        ones_cols = vop(lambda v: v.memset(vaug[:, :, 64:65], 1.0))
        ones_cols = vop(lambda v: v.memset(vaug[:, :, 129:130], 1.0))

        unorm_done = [[0] * IB for _ in range(NPAIR)]
        exp_last = {}
        av_last = {}
        pS_reader = {}
        pU_reader = [0, 0]
        bcast_done = [0, 0]
        qk_evac = {0: 0, 1: 0, 4: 0, 5: 0}
        wgs_used = 0

        for pp_i in range(NPAIR):
            base_q = pp_i * 128
            w = [(SP, wgs_used)] if pp_i > 0 else []
            for si, coff in enumerate([base_q, D + base_q, 2 * D + base_q]):
                wg_ld = dma("gpsimd", "swg", SWG,
                            lambda g, si=si, coff=coff: g.dma_start(
                                wgs[:, si, :, :],
                                wgt_h[:, coff:coff + 128].rearrange(
                                    "(c p) n -> p c n", p=128)),
                            waits=(w if si == 0 else ()))
            for qk in range(2):
                dstT = qTb if qk == 0 else kTb
                for tb in range(8):
                    sl = (qk * 8 + tb) % 2
                    w = [(SWG, wg_ld)]
                    if qk_evac[sl]:
                        w.append((SA, qk_evac[sl]))
                    if pp_i > 0 and qk == 0 and tb <= 1 and bcast_done[sl]:
                        w.append((SV, bcast_done[sl]))
                    if pp_i == 0 and tb == 0:
                        w += fence_pa
                    for kc in range(DC):
                        p.op("tensor", lambda t, sl=sl, qk=qk, kc=kc, tb=tb: t.matmul(
                            pS[:, sl, :], wgs[:, qk, kc, :],
                            a2T[:, kc, tb * 512:(tb + 1) * 512],
                            start=(kc == 0), stop=(kc == DC - 1)),
                            waits=(w if kc == 0 else ()),
                            incs=([(SP, "sp", 1)] if kc == DC - 1 else ()))
                    mm_done = p.cur("sp")
                    qk_evac[sl] = sop(
                        lambda sc, sl=sl, tb=tb, dstT=dstT, qk=qk, pp_i=pp_i: sc.activation(
                            dstT[:, tb * 512:(tb + 1) * 512], pS[:, sl, :],
                            AF.Identity, bias=bgr[:, qk * 8 + pp_i:qk * 8 + pp_i + 1]),
                        waits=[(SP, mm_done)])
            for jt in range(TT):
                sl = 4 + (jt % 2)
                w = [(SWG, wg_ld), (SV, ones_cols)]
                if qk_evac[sl]:
                    w.append((SA, qk_evac[sl]))
                if pp_i > 0 and jt <= 1:
                    prr = pS_reader.get((jt % 2, 2))
                    if prr:
                        w.append((SA if prr[0] == "sa" else SV, prr[1]))
                if pp_i == 0 and jt <= 1:
                    w += [(SA, pa_sa), (SV, p0_sv)]
                for kc in range(DC):
                    p.op("tensor", lambda t, sl=sl, kc=kc, jt=jt: t.matmul(
                        pS[:, sl, 0:128], a2T[:, kc, jt * 128:(jt + 1) * 128],
                        wgs[:, 2, kc, :],
                        start=(kc == 0), stop=(kc == DC - 1)),
                        waits=(w if kc == 0 else ()),
                        incs=([(SP, "sp", 1)] if kc == DC - 1 else ()))
                mm_done = p.cur("sp")
                sop(lambda sc, sl=sl, jt=jt: sc.activation(
                    vaug[:, jt, 0:64], pS[:, sl, 0:64], AF.Copy),
                    waits=[(SP, mm_done)])
                qk_evac[sl] = sop(lambda sc, sl=sl, jt=jt: sc.activation(
                    vaug[:, jt, 65:129], pS[:, sl, 64:128], AF.Copy))
            wgs_used = p.cur("sp")
            qkv_evac_done = p.cur("sa")

            for ib in range(IB):
                jmax = min(4 * ib + 4, TT - 1)

                def emit_S(J):
                    o = J - 4 * ib
                    istart = 0 if o <= 0 else 128 * o - 64
                    N = 512 if o <= 0 else 576 - 128 * o
                    for h in range(2):
                        sl = 2 * (J % 3) + h
                        w = [(SA, qkv_evac_done)]
                        pr = pS_reader.get((h, J % 3))
                        if pr:
                            w.append((SA if pr[0] == "sa" else SV, pr[1]))
                        p.op("tensor", lambda t, h=h, sl=sl, J=J, istart=istart,
                             N=N, ib=ib: t.matmul(
                            pS[:, sl, 0:N],
                            kTb[h * 64:(h + 1) * 64, J * 128:(J + 1) * 128],
                            qTb[h * 64:(h + 1) * 64,
                                ib * 512 + istart:ib * 512 + istart + N],
                            start=True, stop=True,
                            tile_position=(64 * h, 0)),
                            waits=w, incs=[(SP, "sp", 1)])

                def emit_exp(J):
                    o = J - 4 * ib
                    istart = 0 if o <= 0 else 128 * o - 64
                    N = 512 if o <= 0 else 576 - 128 * o
                    sp_done = p.cur("sp")
                    w2 = [(SP, sp_done)]
                    for h in range(2):
                        if av_last.get((h, J % 3), 0):
                            w2.append((SP, av_last[(h, J % 3)]))
                    if o < 0:
                        e_done = sop(lambda sc, J=J: sc.activation(
                            pbuf[:, :, J % 3, :], pS[:, 2 * (J % 3):2 * (J % 3) + 2, :],
                            AF.Exp, scale=0.125), waits=w2)
                        for h in range(2):
                            exp_last[(h, J % 3)] = ("sa", e_done)
                            pS_reader[(h, J % 3)] = ("sa", e_done)
                    else:
                        for h in range(2):
                            e_done = sop(lambda sc, h=h, J=J, N=N: sc.activation(
                                pbuf[:, h, J % 3, 0:N], pS[:, 2 * (J % 3) + h, 0:N],
                                AF.Exp, scale=0.125), waits=w2)
                            e_done2 = vop(lambda v, h=h, J=J, o=o, N=N: v.tensor_mul(
                                pbuf[:, h, J % 3, 0:N], pbuf[:, h, J % 3, 0:N],
                                masks_s[:, o, 0:N]),
                                waits=[(SA, e_done), (SWG, masks_ld)])
                            exp_last[(h, J % 3)] = ("sv", e_done2)
                            pS_reader[(h, J % 3)] = ("sa", e_done)

                def emit_AV(J):
                    o = J - 4 * ib
                    istart = 0 if o <= 0 else 128 * o - 64
                    N = 512 if o <= 0 else 576 - 128 * o
                    for h in range(2):
                        kind, val = exp_last[(h, J % 3)]
                        w = [(SV if kind == "sv" else SA, val)]
                        if J == 0 and pU_reader[h]:
                            k2, v2 = pU_reader[h]
                            w.append((SA if k2 == "sa" else SV, v2))
                        p.op("tensor", lambda t, h=h, J=J, istart=istart, N=N: t.matmul(
                            pUs[h][0:65, istart:istart + N],
                            vaug[:, J, h * 65:h * 65 + 65],
                            pbuf[:, h, J % 3, 0:N],
                            start=(J == 0), stop=False),
                            waits=w, incs=[(SP, "sp", 1)])
                        av_last[(h, J % 3)] = p.cur("sp")

                for J in range(jmax + 1):
                    emit_S(J)
                    emit_exp(J)
                    if J >= 2:
                        emit_AV(J - 2)
                emit_AV(jmax - 1)
                emit_AV(jmax)
                for h in range(2):
                    p.op("tensor", lambda t, h=h: t.matmul(
                        pUs[h][0:65, :], ones_row[0:1, 0:65], zrow[:],
                        start=False, stop=True), incs=[(SP, "sp", 1)])
                    av_last[(h, "close")] = p.cur("sp")
                u_done = [av_last[(0, "close")], av_last[(1, "close")]]

                for h in range(2):
                    ue_done = sop(lambda sc, h=h: sc.activation(
                        usbf[0:65, h, :], pUs[h][0:65, :], AF.Copy),
                        waits=[(SP, u_done[h])])
                    pU_reader[h] = ("sa", ue_done)
                    sop(lambda sc, h=h: sc.activation(
                        lnrow[0:1, h, :], usbf[64:65, h, :], AF.Ln))
                    r_done = sop(lambda sc, h=h: sc.activation(
                        recd[0:1, h, :], lnrow[0:1, h, :], AF.Exp, scale=-1.0))
                    w = [(SA, r_done), (SV, ones_ready)]
                    pr = pS_reader.get((h, 0))
                    if pr:
                        w.append((SA if pr[0] == "sa" else SV, pr[1]))
                    if bcast_done[h]:
                        w.append((SV, bcast_done[h]))
                    p.op("tensor", lambda t, h=h: t.matmul(
                        pS[:, h, :], ones_row[:], recd[0:1, h, :],
                        start=True, stop=True), waits=w, incs=[(SP, "sp", 1)])
                    b_done = p.cur("sp")
                    un = vop(lambda v, h=h, pp_i=pp_i, ib=ib: v.tensor_mul(
                        unorm[h * 64:(h + 1) * 64, pp_i, ib * 512:(ib + 1) * 512],
                        usbf[0:64, h, :], pS[0:64, h, :]),
                        waits=[(SP, b_done)])
                    bcast_done[h] = un
                    pS_reader[(h, 0)] = ("sv", un)
                unorm_done[pp_i][ib] = p.cur("sv")

        pb_sa, pb_sv, pb_sp = p.cur("sa"), p.cur("sv"), p.cur("sp")
        for g in reversed(pb_guards):
            g.__exit__(None, None, None)

        if "unorm" in dbg:
            dma("sync", "sy", SY, lambda s: s.dma_start(dbg["unorm"][:], unorm[:]),
                waits=[(SV, pb_sv)])
        if "qkT" in dbg:
            dma("sync", "sy", SY, lambda s: s.dma_start(dbg["qkT"][:, 0, :], qTb[:]),
                waits=[(SV, pb_sv), (SA, pb_sa)])
            dma("sync", "sy", SY, lambda s: s.dma_start(dbg["qkT"][:, 1, :], kTb[:]))
            dma("sync", "sy", SY, lambda s: s.dma_start(dbg["vaug"][:], vaug[:]))

        pc_guards = []
        attb_g = nc.sbuf_tensor("attb", [128, 2, D], BF16); pc_guards.append(attb_g)
        attb = attb_g.__enter__()
        pw_g = nc.psum_tensor("pw", [128, 2, 512], F32); pc_guards.append(pw_g)
        pw = pw_g.__enter__()
        pcv_g = nc.psum_tensor("pcv", [128, 512], F32); pc_guards.append(pcv_g)
        pcv = pcv_g.__enter__()

        SATT2 = [sems["satt"], sems["satt1"]]
        cv_done = 0
        for dch in range(2):
            w = [(SWO, wot_ld), (SA, pb_sa), (SV, pb_sv), (SW, pers_loaded)]
            if cv_done:
                w.append((SV, cv_done))
            for c in range(DC):
                p.op("tensor", lambda t, c=c, dch=dch: t.matmul(
                    pcv[0:1, :], bgr_bf[:, c:c + 1],
                    wot_sb[:, c, dch * 512:(dch + 1) * 512],
                    start=(c == 0), stop=(c == DC - 1)),
                    waits=(w if c == 0 else ()),
                    incs=([(SP, "sp", 1)] if c == DC - 1 else ()))
            cvm = p.cur("sp")
            cv_done = vop(lambda v, dch=dch: v.tensor_add(
                cvec2[0:1, dch * 512:(dch + 1) * 512], bo_sb[0:1, dch * 512:(dch + 1) * 512],
                pcv[0:1, 0:512]), waits=[(SP, cvm)])

        att_dma = [0, 0]
        attb_evac = [0, 0]
        satt_names = ["satt", "satt1"]
        for tt2 in range(TT):
            par = tt2 % 2
            for dch in range(2):
                w = [(SV, unorm_done[NPAIR - 1][tt2 // 4]), (SV, cv_done)]
                if attb_evac[par]:
                    w.append((SA, attb_evac[par]))
                for pp2 in range(NPAIR):
                    p.op("tensor", lambda t, pp2=pp2, tt2=tt2, dch=dch: t.matmul(
                        pw[:, dch, :], unorm[:, pp2, tt2 * 128:(tt2 + 1) * 128],
                        wot_sb[:, pp2, dch * 512:(dch + 1) * 512],
                        start=(pp2 == 0), stop=False),
                        waits=(w if pp2 == 0 else ()))
                p.op("tensor", lambda t, dch=dch: t.matmul(
                    pw[:, dch, :], ones_row[:], cvec2[0:1, dch * 512:(dch + 1) * 512],
                    start=False, stop=True), incs=[(SP, "sp", 1)])
                wo_done = p.cur("sp")
                w2 = [(SP, wo_done)]
                if dch == 0 and att_dma[par]:
                    w2.append((SATT2[par], att_dma[par]))
                attb_evac[par] = sop(lambda sc, par=par, dch=dch: sc.activation(
                    attb[:, par, dch * 512:(dch + 1) * 512], pw[:, dch, :], AF.Copy),
                    waits=w2)
            att_dma[par] = dma("sync", satt_names[par], SATT2[par],
                               lambda s, tt2=tt2, par=par: s.dma_start(
                                   att_h[tt2 * 128:(tt2 + 1) * 128, :], attb[:, par, :]),
                               waits=[(SA, attb_evac[par])])
        pc_sa, pc_sp, pc_sv = p.cur("sa"), p.cur("sp"), p.cur("sv")

        if "att" in dbg:
            dma("sync", "sy", SY, lambda s: s.dma_start(dbg["att"][:], att_h[:]),
                waits=[(SATT2[0], p.cur("satt")), (SATT2[1], p.cur("satt1"))])

        for g in reversed(pc_guards):
            g.__exit__(None, None, None)

    if stage >= 5:
        for _ in range(5):
            g = ctx_stack.pop()
            g.__exit__(None, None, None)

        SQW = [sems["sqw0"], sems["sqw1"]]
        SFN, SFP = sems["sfn"], sems["sfp"]
        SFX = [sems["sfx0"], sems["sfx1"]]
        SFA = [sems["sfa0"], sems["sfa1"]]
        SYO = [sems["syo0"], sems["syo1"]]

        fe_guards = []

        def sbE(name, shape, dtype):
            g = nc.sbuf_tensor(name, shape, dtype)
            fe_guards.append(g)
            return g.__enter__()

        w1q = sbE("w1q", [128, 2, DC, 1024], BF16)
        w2q = sbE("w2q", [128, 2, DC, 1024], BF16)
        acc = sbE("acc", [128, 8, D], F32)
        nrmhc = sbE("nrmhc", [128, 2, DC, 1024], BF16)
        xeb = sbE("xeb", [128, 2, DC, 512], BF16)
        hTb = sbE("hTb", [128, 2, DC, 512], BF16)
        b2row_s = sbE("b2row_s", [1, NE * D], BF16)
        b1f_s = sbE("b1f_s", [128, NE, HID // 128], F32)
        elns_s = sbE("elns_s", [128, NE, DC], F32)
        elnb_s = sbE("elnb_s", [128, NE, DC], F32)
        xq = sbE("xq", [128, 2, D], F32)
        atq = sbE("atq", [128, 2, D], BF16)

        fence_fe = [(SA, pc_sa), (SV, pc_sv), (SP, pc_sp),
                    (SATT2[0], p.cur("satt")), (SATT2[1], p.cur("satt1"))]

        qw_ld = {}
        qw_used = {}
        nq_total = NH * NE * 4

        def emit_qload(qidx):
            hq, rem = divmod(qidx, NE * 4)
            e, jq = divmod(rem, 4)
            slot = qidx % 2
            w = [(SP, pb_sp)] if qidx < 2 else [(SP, qw_used[qidx - 2])]
            dma("gpsimd", "sqw%d" % slot, SQW[slot],
                lambda g, e=e, jq=jq, slot=slot: g.dma_start(
                    w1q[:, slot, :, :],
                    w1tf_h[e].rearrange("(c p) j -> p c j", p=128)[
                        :, :, jq * 1024:(jq + 1) * 1024]),
                waits=w)
            qw_ld[qidx] = dma("gpsimd", "sqw%d" % slot, SQW[slot],
                lambda g, e=e, jq=jq, slot=slot: g.dma_start(
                    w2q[:, slot, :, :],
                    w2tf_h[e][jq * 1024:(jq + 1) * 1024, :].rearrange(
                        "(c p) d -> p c d", p=128)))

        emit_qload(0)
        emit_qload(1)

        dma("gpsimd", "sfp", SFP, lambda g: g.dma_start(b2row_s[:], b2f_h[:]),
            waits=fence_fe)
        dma("gpsimd", "sfp", SFP, lambda g: g.dma_start(
            b1f_s[:], b1f_h.rearrange("e p c -> p e c")))
        dma("gpsimd", "sfp", SFP, lambda g: g.dma_start(
            elns_s[:], elns_h.rearrange("e p c -> p e c")))
        ffp_ld = dma("gpsimd", "sfp", SFP, lambda g: g.dma_start(
            elnb_s[:], elnb_h.rearrange("e p c -> p e c")))

        nh_ld = [0] * NH
        q_last_xeT = [0] * NH

        def emit_nrmhc_load(hq):
            w = list(fence_fe) if hq < 2 else [(SV, q_last_xeT[hq - 2])]
            nh_ld[hq] = dma("scalar", "sfn", SFN,
                            lambda sc_, hq=hq: sc_.dma_start(
                                nrmhc[:, hq % 2, :, :],
                                nrm_h[:, :, hq * QT:(hq + 1) * QT]), waits=w)

        emit_nrmhc_load(0)

        acc_init_done = [0] * 8
        y_stored = [{} for _ in range(2)]
        xslot_reader = [0] * 2
        xa_ld = {}

        def emit_acc_dmas(hq, tls):
            tbase = hq * QT
            for tl in tls:
                px = tl % 2
                pq = tl % 2
                tg = tbase + tl * 128
                w = []
                if hq:
                    w.append((SYO[pq], y_stored[pq][tl]))
                elif tl < 2:
                    w = list(fence_fe)
                if xslot_reader[px]:
                    w.append((SV, xslot_reader[px]))
                xl = dma("sync", "sfx%d" % pq, SFX[pq],
                         lambda s_, tg=tg, px=px: s_.dma_start(
                             xq[:, px, :], x_h[tg:tg + 128, :]), waits=w)
                al = dma("sync", "sfa%d" % pq, SFA[pq],
                         lambda s_, tg=tg, px=px: s_.dma_start(
                             atq[:, px, :], att_h[tg:tg + 128, :]))
                xa_ld[tl] = (pq, xl, al)

        def emit_acc_adds(tls):
            for tl in tls:
                px = tl % 2
                pq, xl, al = xa_ld[tl]
                acc_init_done[tl] = vop(
                    lambda v, tl=tl, px=px: v.tensor_add(
                        acc[:, tl, :], xq[:, px, :], atq[:, px, :]),
                    waits=[(SFX[pq], xl), (SFA[pq], al)])
                xslot_reader[px] = acc_init_done[tl]

    if stage >= 4:
        SRN = [sems["srn0"], sems["srn1"]]
        pd_guards = []

        def sbD(name, shape, dtype):
            g = nc.sbuf_tensor(name, shape, dtype)
            pd_guards.append(g)
            return g.__enter__()

        nrmr = sbD("nrmr", [128, 2, DC, 128], BF16)
        rlnt = sbD("rlnt", [128, 2, DC, 128], BF16)
        lbuf = sbD("lbuf", [128, TT, NE], F32)
        qbuf = sbD("qbuf", [128, TT, NE], F32)
        mbuf = sbD("mbuf", [128, TT, NE], F32)
        dbuf = sbD("dbuf", [128, TT, NE], F32)
        wrk = sbD("wrk", [128, TT, 4], F32)
        pr_g = nc.psum_tensor("pr", [128, 2, 512], F32)
        pd_guards.append(pr_g)
        pr = pr_g.__enter__()

        fence_pc = [(SA, pc_sa), (SV, pc_sv), (SP, pc_sp),
                    (SATT2[0], p.cur("satt")), (SATT2[1], p.cur("satt1"))]
        rln_done = [0, 0]
        rmm_done = [0, 0]
        lcp_done = {}

        def emit_rln(t):
            sl2 = t % 2
            w = list(fence_pc) if t < 2 else [(SV, rln_done[sl2])]
            ld = dma("sync", "srn%d" % sl2, SRN[sl2],
                     lambda s, t=t, sl2=sl2: s.dma_start(
                         nrmr[:, sl2, :, :],
                         nrm_h[:, :, t * 128:(t + 1) * 128]),
                     waits=w)
            w = [(SRN[sl2], ld)]
            if rmm_done[sl2]:
                w.append((SP, rmm_done[sl2]))
            for kc in range(DC):
                rln_done[sl2] = vop(lambda v, sl2=sl2, kc=kc: v.tensor_scalar(
                    rlnt[:, sl2, kc, :], nrmr[:, sl2, kc, :],
                    rlns_s[:, kc:kc + 1], rlnb_s[:, kc:kc + 1],
                    ALU.mult, ALU.add),
                    waits=(w if kc == 0 else ()))

        def emit_lmm(t):
            sl2 = t % 2
            bk = t % 2
            w = [(SV, rln_done[sl2]), (SW, pers_loaded)]
            if lcp_done.get(t - 2) is not None:
                w.append((SV, lcp_done[t - 2]))
            if t < 2:
                w += [(SA, pc_sa), (SV, pc_sv)]
            for kc in range(DC):
                p.op("tensor", lambda t_, bk=bk, sl2=sl2, kc=kc: t_.matmul(
                    pr[:, bk, 0:8], rlnt[:, sl2, kc, :], rwt_s[:, kc, :],
                    start=(kc == 0), stop=False),
                    waits=(w if kc == 0 else ()))
            p.op("tensor", lambda t_, bk=bk: t_.matmul(
                pr[:, bk, 0:8], ones_row[:], rb_s[0:1, :],
                start=False, stop=True), incs=[(SP, "sp", 1)])
            rmm_done[sl2] = p.cur("sp")
            lcp_done[t] = vop(lambda v, t=t, bk=bk: v.tensor_copy(
                lbuf[:, t, :], pr[:, bk, 0:8]), waits=[(SP, rmm_done[sl2])])

        emit_rln(0)
        for t in range(TT):
            if t + 1 < TT:
                emit_rln(t + 1)
            emit_lmm(t)

        vop(lambda v: v.reduce_max(wrk[:, :, 0:1], lbuf[:], axis=AX.X))
        sub_done = vop(lambda v: v.tensor_sub(
            mbuf[:], lbuf[:], wrk[:, :, 0:1].broadcast_to([128, TT, NE])))
        exp_done = sop(lambda sc: sc.activation(qbuf[:], mbuf[:], AF.Exp),
                       waits=[(SV, sub_done)])
        vop(lambda v: v.tensor_scalar(
            mbuf[:], qbuf[:], 1.0, -1e30, ALU.is_ge, ALU.mult),
            waits=[(SA, exp_done)])
        vop(lambda v: v.tensor_add(mbuf[:], qbuf[:], mbuf[:]))
        vop(lambda v: v.reduce_max(wrk[:, :, 1:2], mbuf[:], axis=AX.X))
        vop(lambda v: v.tensor_sub(
            dbuf[:], qbuf[:], wrk[:, :, 1:2].broadcast_to([128, TT, NE])))
        vop(lambda v: v.tensor_scalar(
            dbuf[:], dbuf[:], 0.0, None, ALU.is_ge))
        vop(lambda v: v.tensor_mul(qbuf[:], qbuf[:], dbuf[:]))
        vop(lambda v: v.reduce_sum(wrk[:, :, 2:3], qbuf[:], axis=AX.X))
        vop(lambda v: v.reciprocal(wrk[:, :, 3:4], wrk[:, :, 2:3]))
        vop(lambda v: v.tensor_mul(
            gates[:], qbuf[:], wrk[:, :, 3:4].broadcast_to([128, TT, NE])))

        pd_sa, pd_sv, pd_sp = p.cur("sa"), p.cur("sv"), p.cur("sp")

        if "gates" in dbg:
            dma("sync", "sy", SY, lambda s: s.dma_start(dbg["gates"][:], gates[:]),
                waits=[(SV, pd_sv)])

        for g in reversed(pd_guards):
            g.__exit__(None, None, None)

    if stage >= 5:
        pm1_g = nc.psum_tensor("pm1", [128, 2, 512], F32)
        fe_guards.append(pm1_g); pm1 = pm1_g.__enter__()
        pys = []
        for _i in range(6):
            _g = nc.psum_tensor("py%d" % _i, [128, 512], F32)
            fe_guards.append(_g); pys.append(_g.__enter__())

        xeT_done = {}
        xeb_used_k = {}
        pm1_evac = [0, 0]
        py_evac = {}
        hT_used = [0, 0]
        acc_evac = {}

        iters = [(hq, e, jq, tb)
                 for hq in range(NH) for e in range(NE)
                 for jq in range(4) for tb in range(2)]
        NIT = len(iters)

        def emit_xeT(k):
            hq, e, jq, tb = iters[k]
            xsl = k % 2
            w = [(SFN, nh_ld[hq]), (SFP, ffp_ld)]
            if k >= 2 and xeb_used_k.get(k - 2):
                w.append((SP, xeb_used_k[k - 2]))
            for kc in range(DC):
                vop(lambda v, hq=hq, e=e, kc=kc, tb=tb, xsl=xsl: v.tensor_scalar(
                    xeb[:, xsl, kc, :],
                    nrmhc[:, hq % 2, kc, tb * 512:(tb + 1) * 512],
                    elns_s[:, e, kc:kc + 1], elnb_s[:, e, kc:kc + 1],
                    ALU.mult, ALU.add),
                    waits=(w if kc == 0 else ()))
            xeT_done[k] = p.cur("sv")
            q_last_xeT[hq] = xeT_done[k]

        emit_nrmhc_load(1)
        emit_xeT(0)
        emit_acc_dmas(0, range(2))
        emit_acc_adds(range(2))
        for _tl in range(2, 8):
            emit_acc_dmas(0, [_tl])
            emit_acc_adds([_tl])

        for k in range(NIT):
            hq, e, jq, tb = iters[k]
            qidx = (hq * NE + e) * 4 + jq
            slot = qidx % 2
            xsl = k % 2
            if k + 1 < NIT:
                emit_xeT(k + 1)
            for js in range(DC):
                sl1 = js % 2
                w = [(SV, xeT_done[k]), (SQW[slot], qw_ld[qidx])]
                if pm1_evac[sl1]:
                    w.append((SA, pm1_evac[sl1]))
                for kc in range(DC):
                    p.op("tensor",
                         lambda t, sl1=sl1, js=js, kc=kc, slot=slot, xsl=xsl:
                         t.matmul(
                             pm1[:, sl1, :],
                             w1q[:, slot, kc, js * 128:(js + 1) * 128],
                             xeb[:, xsl, kc, :],
                             start=(kc == 0), stop=(kc == DC - 1)),
                         waits=(w if kc == 0 else ()),
                         incs=([(SP, "sp", 1)] if kc == DC - 1 else ()))
                m1_done = p.cur("sp")
                gw = [(SP, m1_done)]
                if js == 0 and hT_used[tb]:
                    gw.append((SP, hT_used[tb]))
                pm1_evac[sl1] = sop(
                    lambda sc, sl1=sl1, js=js, tb=tb, e=e, jq=jq:
                    sc.activation(
                        hTb[:, tb, js, :], pm1[:, sl1, :], AF.Gelu,
                        bias=b1f_s[:, e, jq * 8 + js:jq * 8 + js + 1]),
                    waits=gw)
            xeb_used_k[k] = p.cur("sp")
            h_done = p.cur("sa")
            for gi in range(4):
                dch, tp = divmod(gi, 2)
                bsel = (gi % 3) * 2
                w = [(SA, h_done)]
                if py_evac.get(bsel):
                    w.append((SV, py_evac[bsel]))
                first = True
                for js in range(DC):
                    for ts in range(2):
                        tsub = tp * 2 + ts
                        last = (js == DC - 1 and ts == 1 and jq != 3)
                        p.op("tensor",
                             lambda t, js=js, tsub=tsub, dch=dch,
                             bsel=bsel, ts=ts, slot=slot, tb=tb, jq=jq:
                             t.matmul(
                                 pys[bsel + ts][:],
                                 hTb[:, tb, js,
                                     tsub * 128:(tsub + 1) * 128],
                                 w2q[:, slot, js,
                                     dch * 512:(dch + 1) * 512],
                                 start=(js == 0),
                                 stop=(js == DC - 1 and jq != 3)),
                             waits=(w if first else ()),
                             incs=([(SP, "sp", 1)] if last else ()))
                        first = False
                if jq == 3:
                    for ts in range(2):
                        p.op("tensor", lambda t, bsel=bsel, ts=ts,
                             e=e, dch=dch: t.matmul(
                                 pys[bsel + ts][:], ones_row[:],
                                 b2row_s[0:1, e * D + dch * 512:
                                         e * D + (dch + 1) * 512],
                                 start=False, stop=True),
                             incs=([(SP, "sp", 1)] if ts == 1 else ()))
                g_done = p.cur("sp")
                if gi == 3:
                    hT_used[tb] = g_done
                for ts in range(2):
                    tsub = tp * 2 + ts
                    tl = tb * 4 + tsub
                    ttg = hq * 8 + tl
                    w2 = [(SP, g_done)]
                    if e == 0 and jq == 0:
                        w2.append((SV, acc_init_done[tl]))
                    py_evac[bsel] = vop(
                        lambda v, bsel=bsel, ts=ts, tl=tl, ttg=ttg,
                        dch=dch, e=e: v.scalar_tensor_tensor(
                            acc[:, tl, dch * 512:(dch + 1) * 512],
                            pys[bsel + ts][:],
                            gates[:, ttg, e:e + 1],
                            acc[:, tl, dch * 512:(dch + 1) * 512],
                            ALU.mult, ALU.add),
                        waits=w2)
                    if e == NE - 1 and jq == 3:
                        acc_evac[(tl, dch)] = py_evac[bsel]
            qw_used[qidx] = p.cur("sp")
            if tb == 1 and qidx + 2 < nq_total:
                emit_qload(qidx + 2)
            if tb == 1 and jq == 3 and e == NE - 1:
                tbase = hq * QT
                for tl in range(8):
                    par = tl % 2
                    tg = tbase + tl * 128
                    y_stored[par][tl] = dma(
                        "sync", "syo%d" % par, SYO[par],
                        lambda sy, tg=tg, tl=tl: sy.dma_start(
                            y_h[tg:tg + 128, :], acc[:, tl, :]),
                        waits=[(SV, acc_evac[(tl, 0)]), (SV, acc_evac[(tl, 1)])])
                if hq + 1 < NH:
                    for _tl in range(8):
                        emit_acc_dmas(hq + 1, [_tl])
                        emit_acc_adds([_tl])
                if hq + 2 < NH:
                    emit_nrmhc_load(hq + 2)

        for g in reversed(fe_guards):
            g.__exit__(None, None, None)

    if stage < 5:
        dma("gpsimd", "sdum", sems["sdum"], lambda g: g.dma_start(
            y_h[0:128, 0:128], ident[:]),
            waits=[(SA, p.cur("sa")), (SV, p.cur("sv"))])

    for nm, sem in sems.items():
        if nm in ("sa", "sv", "sp") or p.cur(nm) == 0:
            continue
        p.wait("sync", sem, p.cur(nm))

    with nc.Block() as block:
        block.tensor(lambda t: [f(t) for f in p.ops["tensor"]])
        block.scalar(lambda sc: [f(sc) for f in p.ops["scalar"]])
        block.vector(lambda v: [f(v) for f in p.ops["vector"]])
        block.sync(lambda s: [f(s) for f in p.ops["sync"]])
        block.gpsimd(lambda g: [f(g) for f in p.ops["gpsimd"]])

    for g in reversed(p0_guards):
        g.__exit__(None, None, None)
    for g in reversed(ctx_stack):
        g.__exit__(None, None, None)
    return nc


def _prep_host(inputs):
    A = np.ascontiguousarray
    com = {
        "w1t_a": A(inputs["L0_attn_W1"].T),
        "b1_a": A(np.reshape(inputs["L0_attn_b1"], (DC, 128)).T),
        "w2t_a": A(inputs["L0_attn_W2"].T),
        "b2_a": A(np.reshape(inputs["L0_attn_b2"], (DC, 128)).T),
        "wgt": A(inputs["L0_attn_Wg"].T),
        "bg_r": A(np.reshape(inputs["L0_attn_bg"], (24, 128)).T),
        "wot": A(inputs["L0_attn_Wo"].T),
        "bo_row": A(np.reshape(inputs["L0_attn_bo"], (1, D))),
        "rlns": A(np.reshape(inputs["L0_ffn_router_ln_scale"], (DC, 128)).T),
        "rlnb": A(np.reshape(inputs["L0_ffn_router_ln_bias"], (DC, 128)).T),
        "rwt": A(inputs["L0_ffn_router_w"].T),
        "rb_row": A(np.reshape(inputs["L0_ffn_router_b"], (1, NE))),
        "elns": A(np.reshape(inputs["L0_ffn_expert_ln_scale"], (NE, DC, 128)).transpose(0, 2, 1)),
        "elnb": A(np.reshape(inputs["L0_ffn_expert_ln_bias"], (NE, DC, 128)).transpose(0, 2, 1)),
        "w1t_f": A(inputs["L0_ffn_w1"].transpose(0, 2, 1)),
        "b1_f": A(np.reshape(inputs["L0_ffn_b1"], (NE, HID // 128, 128)).transpose(0, 2, 1)),
        "w2t_f": A(inputs["L0_ffn_w2"].transpose(0, 2, 1)),
        "b2_row": A(np.reshape(inputs["L0_ffn_b2"], (1, NE * D))),
    }
    xs = np.asarray(inputs["L0_attn_x"])
    return [dict(com, x=A(xs[b])) for b in range(8)]


def kernel(**inputs):
    in_maps = _prep_host({k: np.asarray(v) for k, v in inputs.items()})
    nc = build_nc()
    res = run_bass_kernel_spmd(nc, in_maps, core_ids=list(range(8)))
    out = np.stack([np.asarray(r["y"]) for r in res.results], axis=0)
    return out.astype(np.float32)
